# version 12
# baseline (speedup 1.0000x reference)
import numpy as np
import ml_dtypes
import concourse.bass as bass
import concourse.mybir as mybir
from concourse.bass_utils import run_bass_kernel_spmd

F32 = mybir.dt.float32
BF16 = mybir.dt.bfloat16
AF = mybir.ActivationFunctionType
ALU = mybir.AluOpType
AX = mybir.AxisListType

B, S, C = 4, 2048, 2048
NH, DH = 16, 128
NEXP, CI = 32, 768
TOK = 1024
NIC = C // 128
NICC = CI // 128
NGRP = 4
EPS_LN = 1e-5
EPS_L2 = 1e-8
NEG_BIG = -1e30

KB = 1024


class Plan:
    def __init__(self, nc, sem_names):
        self.nc = nc
        self.q = {k: [] for k in ("sync", "gpsimd", "tensor", "vector", "scalar")}
        self.sems = {}
        self.cnt = {}
        self._sem_guards = []
        for nm in sem_names:
            g = nc.semaphore(nm)
            h = g.__enter__()
            self._sem_guards.append(g)
            self.sems[nm] = h
            self.cnt[nm] = 0

    def emit(self, eng, fn):
        self.q[eng].append(fn)

    def wait(self, eng, sem_name, val):
        if val <= 0:
            return
        h = self.sems[sem_name]
        self.q[eng].append(lambda e: e.wait_ge(h, val))

    def inc(self, sem_name, n=1):
        self.cnt[sem_name] += n
        return self.cnt[sem_name]


def build_nc():
    nc = bass.Bass("TRN2", target_bir_lowering=False, debug=False)

    x_h = nc.dram_tensor("x", [2 * TOK, C], F32, kind="ExternalInput").ap()
    wqt_h = nc.dram_tensor("wqt", [C, C], F32, kind="ExternalInput").ap()
    wkt_h = nc.dram_tensor("wkt", [C, C], F32, kind="ExternalInput").ap()
    wvt_h = nc.dram_tensor("wvt", [C, C], F32, kind="ExternalInput").ap()
    gwt_h = nc.dram_tensor("gwt", [128, NIC * NEXP], F32, kind="ExternalInput").ap()
    tau_h = nc.dram_tensor("tau", [1, 1], F32, kind="ExternalInput").ap()
    nu_h = nc.dram_tensor("nu", [1, 1], F32, kind="ExternalInput").ap()
    guw_h = nc.dram_tensor("guw", [NEXP, NICC, 128, NIC * 2 * 128], F32, kind="ExternalInput").ap()
    wd_h = nc.dram_tensor("wd", [NEXP, CI, C], F32, kind="ExternalInput").ap()
    tri_h = nc.dram_tensor("trimask", [128, 128], BF16, kind="ExternalInput").ap()
    m8_h = nc.dram_tensor("mask8", [128, TOK], BF16, kind="ExternalInput").ap()
    m12_h = nc.dram_tensor("mask12", [128, 512], BF16, kind="ExternalInput").ap()
    out_h = nc.dram_tensor("out", [TOK, C], F32, kind="ExternalOutput").ap()

    A = nc.alloc_sbuf_tensor
    ident_bf = A("ident_bf", [128, 128], BF16)
    ident_f32 = A("ident_f32", [128, 128], F32)
    tri_sb = A("tri_sb", [128, 128], BF16)
    m8_sb = A("m8_sb", [128, TOK], BF16)
    m12_sb = A("m12_sb", [128, 512], BF16)
    onesc = A("onesc", [1, 128], F32)
    onesc_bf = A("onesc_bf", [1, 128], BF16)
    onescol = A("onescol", [128, 1], BF16)
    tau_sb = A("tau_sb", [1, 1], F32)
    nu_sb = A("nu_sb", [1, 1], F32)
    negat1 = A("negat1", [1, 1], F32)
    nurow = A("nurow", [1, 128], F32)
    tau_col = A("tau_col", [128, 1], F32)
    negtau_col = A("negtau_col", [128, 1], F32)
    eps_col = A("eps_col", [128, 1], F32)
    gwt_sb = A("gwt_sb", [128, NIC * NEXP], BF16)
    wT_sb = A("wT_sb", [NEXP, TOK], F32)
    nmr = [A(f"nmr{i}", [128, 1], F32) for i in range(2)]
    negm = [A(f"negm{i}", [128, 1], F32) for i in range(2)]
    msq = [A(f"msq{i}", [128, 1], F32) for i in range(2)]
    varb = [A(f"varb{i}", [128, 1], F32) for i in range(2)]
    stdb = [A(f"stdb{i}", [128, 1], F32) for i in range(2)]
    rcol = [A(f"rcol{i}", [128, 1], F32) for i in range(2)]
    negmr = [A(f"negmr{i}", [128, 1], F32) for i in range(2)]
    ssq_ln = [A(f"ssq_ln{i}", [128, 1], F32) for i in range(2)]
    ssq4 = [A(f"ssq4_{i}", [128, 4], F32) for i in range(4)]
    std4 = [A(f"std4_{i}", [128, 4], F32) for i in range(4)]
    rec4 = [A(f"rec4_{i}", [128, 4], F32) for i in range(4)]
    probs_t = [A(f"probs{i}", [128, NEXP], F32) for i in range(2)]
    w_t = [A(f"wt{i}", [128, NEXP], F32) for i in range(2)]
    top8 = A("top8", [128, 8], F32)
    theta = A("theta", [128, 1], F32)
    dnm = A("dnm", [128, 1], F32)
    rdn = A("rdn", [128, 1], F32)
    negmax = [A(f"negmax{i}", [128, 1], F32) for i in range(2)]
    rec = A("rec", [1, TOK], F32)
    kq_tok = [A(f"kq_tok{i}", [128, 512], F32) for i in range(3)]
    sq_junk = A("sq_junk", [128, C], BF16)

    auto_high = 229376 - nc.sbuf_bytes_remaining
    assert auto_high <= 44 * KB - 256, f"autos too big: high water {auto_high}"
    AT = nc.alloc_sbuf_tensor_at
    xn_own = AT("xn_own", [128, NIC * TOK], BF16, offset=44 * KB)
    attn_bf = AT("attn_bf", [128, NIC * TOK], BF16, offset=76 * KB)
    wd_sb = AT("wd_sb", [128, NICC * C], BF16, offset=76 * KB)
    x_sb = [AT(f"x_sb{i}", [128, C], F32, offset=(108 + 8 * i) * KB) for i in range(2)]
    scratch = AT("scratch", [128, C], BF16, offset=212 * KB)
    xn_oth = AT("xn_oth", [128, NIC * TOK], BF16, offset=124 * KB)
    wt_sb1 = AT("wt_sb1", [128, NIC * 512], BF16, offset=108 * KB)
    kT_g = AT("kT_g", [128, 4 * 2048], BF16, offset=156 * KB)
    v_g = AT("v_g", [128, NIC * 512], BF16, offset=172 * KB)
    qT_g = AT("qT_g", [128, 4 * TOK], BF16, offset=188 * KB)
    wt_sb0 = AT("wt_sb0", [128, NIC * 512], BF16, offset=196 * KB)
    recb_w = AT("recb_w", [128, TOK], F32, offset=216 * KB)
    wt_sb = [wt_sb0, wt_sb1]
    acc = AT("acc", [128, NIC * TOK], F32, offset=108 * KB)
    guw_sb = [AT(f"guw_sb{i}", [128, NIC * 2 * 128], BF16, offset=(172 + 8 * i) * KB) for i in range(2)]
    gu_sb = [AT(f"gu_sb{i}", [128, NICC * TOK], BF16, offset=(188 + 12 * i) * KB) for i in range(2)]
    gact = AT("gact", [128, TOK], BF16, offset=212 * KB)
    wb_sb = [AT(f"wb_sb{i}", [128, TOK], BF16, offset=(214 + 2 * i) * KB) for i in range(2)]
    wr_sb = [AT(f"wr_sb{i}", [1, TOK], BF16, offset=(218 + 2 * i) * KB) for i in range(2)]
    x2_sb = [AT(f"x2_sb{i}", [128, C], F32, offset=(172 + 8 * i) * KB) for i in range(2)]
    out_sb = [AT(f"out_sb{i}", [128, C], F32, offset=(188 + 8 * i) * KB) for i in range(2)]

    PQ = [nc.alloc_psum_tensor(f"PQ{i}", [128, 1024], F32) for i in range(4)]
    P = [PQ[i // 2][:, (i % 2) * 512:(i % 2) * 512 + 512] for i in range(8)]

    sem_names = (
        "t", "v", "s", "g",
        "dx0", "dx1",
        "dm",
        "dg",
        "wt0", "wt1",
        "gw0", "gw1",
        "dwd",
        "wr0", "wr1",
        "st",
    )
    pl = Plan(nc, sem_names)
    SM = pl.sems

    def setup():
        def f_sync(e):
            e.dma_start(out=tri_sb[:], in_=tri_h).then_inc(SM["dm"], 16)
            e.dma_start(out=m8_sb[:], in_=m8_h).then_inc(SM["dm"], 16)
            e.dma_start(out=m12_sb[:], in_=m12_h).then_inc(SM["dm"], 16)
            e.dma_start(out=tau_sb[:], in_=tau_h).then_inc(SM["dm"], 16)
            e.dma_start(out=nu_sb[:], in_=nu_h).then_inc(SM["dm"], 16)
        pl.emit("sync", f_sync)
        for _ in range(5):
            pl.inc("dm", 16)

        def f_g(e):
            e.memset(ident_bf[:], 0.0)
            e.drain()
            e.affine_select(out=ident_bf[:], in_=ident_bf[:], compare_op=ALU.not_equal,
                            fill=1.0, base=0, pattern=[[-1, 128]], channel_multiplier=1)
            e.memset(ident_f32[:], 0.0)
            e.drain()
            e.affine_select(out=ident_f32[:], in_=ident_f32[:], compare_op=ALU.not_equal,
                            fill=1.0, base=0, pattern=[[-1, 128]], channel_multiplier=1)
            e.memset(onesc[:], 1.0)
            e.memset(onesc_bf[:], 1.0)
            e.memset(onescol[:], 1.0)
            e.memset(eps_col[:], EPS_LN).then_inc(SM["g"], 1)
            e.dma_start(out=gwt_sb[:], in_=gwt_h).then_inc(SM["dg"], 16)
        pl.emit("gpsimd", f_g)
        pl.inc("g", 1)
        pl.inc("dg", 16)
        pl.wait("tensor", "g", 1)
        pl.wait("scalar", "g", 1)
        pl.wait("vector", "g", 1)

        pl.wait("vector", "dm", pl.cnt["dm"])
        def f_v(e):
            e.tensor_scalar_mul(negat1[:], tau_sb[:], -1.0)
            e.tensor_scalar_mul(nurow[:], onesc[:], nu_sb[:])
            e.drain()
            e.tensor_tensor(negat1[:], negat1[:], tau_sb[:], ALU.min).then_inc(SM["v"], 1)
        pl.emit("vector", f_v)
        v_tau = pl.inc("v")

        pl.wait("tensor", "v", v_tau)
        def f_t(e):
            e.matmul(P[6][:, 0:1], lhsT=onesc[:], rhs=tau_sb[:], start=True, stop=True)
            e.matmul(P[6][:, 1:2], lhsT=onesc[:], rhs=negat1[:], start=True, stop=True).then_inc(SM["t"], 1)
        pl.emit("tensor", f_t)
        t_bc = pl.inc("t")
        pl.wait("vector", "t", t_bc)
        def f_v2(e):
            e.tensor_copy(tau_col[:], P[6][:, 0:1])
            e.tensor_copy(negtau_col[:], P[6][:, 1:2]).then_inc(SM["v"], 1)
        pl.emit("vector", f_v2)
        return pl.inc("v")

    v_consts = setup()

    ln_state = {}
    last_tr_evac = {0: None, 1: None}
    def phaseA(tile_hook=None):
        tr_evac_v = {}
        norm_v = {}
        sq_s = {}
        x_dw = {}
        pendA = []

        def emit_A_tr(i):
            par = i % 2
            pl.wait("tensor", "v", norm_v[i])
            for gi in range(4):
                if last_tr_evac[gi % 2]:
                    pl.wait("tensor", *last_tr_evac[gi % 2])
                def f_tr(e, i=i, gi=gi, par=par):
                    pb = P[4 + gi % 2]
                    for j in range(4):
                        ic = gi * 4 + j
                        last = e.transpose(pb[:, j * 128:(j + 1) * 128],
                                           x_sb[par][:, ic * 128:(ic + 1) * 128], ident_f32[:])
                    last.then_inc(SM["t"], 1)
                pl.emit("tensor", f_tr)
                t_tr = pl.inc("t")
                if gi == 3:
                    ln_state[("tr_t", i)] = t_tr
                eng = "vector" if gi % 2 == 0 else "scalar"
                pl.wait(eng, "t", t_tr)
                def f_ev(e, i=i, gi=gi, eng=eng):
                    dstt = xn_own if i < 8 else xn_oth
                    tc = i % 8
                    dst = dstt[:].rearrange("p (ic t) -> p ic t", t=TOK)[:, gi * 4:(gi + 1) * 4, tc * 128:tc * 128 + 128]
                    srcp = P[4 + gi % 2][:].rearrange("p (ic t) -> p ic t", t=128)
                    if eng == "vector":
                        e.tensor_copy(dst, srcp).then_inc(SM["v"], 1)
                    else:
                        e.drain()
                        e.activation(dst, srcp, AF.Copy).then_inc(SM["s"], 1)
                pl.emit(eng, f_ev)
                if eng == "vector":
                    val = ("v", pl.inc("v"))
                else:
                    val = ("s", pl.inc("s"))
                tr_evac_v[i * 4 + gi] = val
                last_tr_evac[gi % 2] = val
        for i in range(16):
            par = i % 2
            sl = f"dx{par}"
            if i >= 2:
                pl.wait("sync", "t", ln_state[("tr_t", i - 2)])
                pl.wait("sync", "s", sq_s[i - 2])
            def f_dx(e, i=i, par=par, sl=sl):
                e.dma_start(out=x_sb[par][:], in_=x_h[i * 128:(i + 1) * 128, :]).then_inc(SM[sl], 16)
            pl.emit("sync", f_dx)
            x_dw[i] = pl.inc(sl, 16)

            pl.wait("scalar", sl, x_dw[i])
            def f_sq(e, i=i, par=par):
                e.drain()
                e.activation(sq_junk[:], x_sb[par][:], AF.Square, accum_out=ssq_ln[par][:])
                e.drain()
                e.activation(sq_junk[:], x_sb[par][:], AF.Copy, accum_out=nmr[par][:]).then_inc(SM["s"], 1)
            pl.emit("scalar", f_sq)
            sq_s[i] = pl.inc("s")

            pl.wait("vector", "s", sq_s[i])
            def f_st2(e, i=i, par=par):
                e.tensor_scalar_mul(negm[par][:], nmr[par][:], -1.0 / C)
                e.drain()
                e.tensor_mul(msq[par][:], negm[par][:], negm[par][:])
                e.drain()
                e.tensor_scalar(varb[par][:], ssq_ln[par][:], 1.0 / C, None, ALU.mult)
                e.drain()
                e.tensor_sub(varb[par][:], varb[par][:], msq[par][:]).then_inc(SM["v"], 1)
            pl.emit("vector", f_st2)
            v_var = pl.inc("v")

            pl.wait("scalar", "v", v_var)
            def f_sd(e, i=i, par=par):
                e.drain()
                e.activation(stdb[par][:], varb[par][:], AF.Sqrt, bias=eps_col[:]).then_inc(SM["s"], 1)
            pl.emit("scalar", f_sd)
            s_std = pl.inc("s")

            pl.wait("vector", "s", s_std)
            def f_nm(e, i=i, par=par):
                e.reciprocal(rcol[par][:], stdb[par][:])
                e.drain()
                e.tensor_mul(negmr[par][:], negm[par][:], rcol[par][:])
                e.drain()
                e.tensor_scalar(x_sb[par][:], x_sb[par][:], rcol[par][:], negmr[par][:],
                                ALU.mult, ALU.add).then_inc(SM["v"], 1)
            pl.emit("vector", f_nm)
            norm_v[i] = pl.inc("v")

            pendA.append(i)
            if len(pendA) > 1:
                done = pendA.pop(0)
                emit_A_tr(done)
                if tile_hook:
                    tile_hook(done, tr_evac_v[done * 4 + 2], tr_evac_v[done * 4 + 3])
        while pendA:
            done = pendA.pop(0)
            emit_A_tr(done)
            if tile_hook:
                tile_hook(done, tr_evac_v[done * 4 + 2], tr_evac_v[done * 4 + 3])
        ln_state["done_v"] = pl.cnt["v"]
        ln_state["done_s"] = pl.cnt["s"]
        ln_state["done_t"] = ln_state[("tr_t", 15)]
        ln_state["last_x_v"] = norm_v[15]
        ln_state["last_x_s"] = sq_s[15]
    W_SRC = [wkt_h, wvt_h, wqt_h]
    wt_dw = {}
    proj_done_t = {}
    attn_done = {}

    def emit_wload(g, j):
        jw = 3 * g + j
        slot = jw % 2
        sl = f"wt{slot}"
        src = W_SRC[j]
        if jw == 1:
            pl.wait("gpsimd", "t", ln_state["done_t"])
            pl.wait("gpsimd", "s", ln_state["last_x_s"])
        if jw >= 2:
            pl.wait("gpsimd", "t", proj_done_t[jw - 2])
        def f(e, g=g, src=src, slot=slot, sl=sl):
            for ic in range(NIC):
                e.dma_start(out=wt_sb[slot][:, ic * 512:(ic + 1) * 512],
                            in_=src[ic * 128:(ic + 1) * 128, g * 512:(g + 1) * 512]).then_inc(SM[sl], 16)
        pl.emit("gpsimd", f)
        for _ in range(NIC):
            pl.inc(sl, 16)
        wt_dw[jw] = (sl, pl.cnt[sl])

    def xn_chunk(ic, tc):
        t_ = xn_own if tc < 8 else xn_oth
        tcl = tc % 8
        return t_[:, ic * TOK + tcl * 128: ic * TOK + tcl * 128 + 128]


    qr_of = lambda kt: (kt * 128, TOK) if kt < 8 else ((0, TOK) if kt < 12 else (512, TOK))
    pT_slots = [scratch[:, 0:TOK], scratch[:, TOK:2 * TOK], sq_junk[:, 0:TOK], sq_junk[:, TOK:2 * TOK]]
    last_kq_read_t = {0: 0, 1: 0, 2: 0}
    pending_epi = []
    pending_rb_t = [0]
    pending_tr = []

    def emit_kq_transpose():
        j0, tc0, v_ev0 = pending_tr.pop(0)
        par2 = tc0 % 2
        if last_tr_evac[par2]:
            pl.wait("tensor", *last_tr_evac[par2])
        pl.wait("tensor", "v", v_ev0)
        def f_tr(e, tc0=tc0):
            par = tc0 % 2
            for h in range(4):
                last = e.transpose(P[4 + par][:, h * 128:(h + 1) * 128],
                                   kq_tok[tc0 % 3][:, h * 128:(h + 1) * 128], ident_f32[:])
            last.then_inc(SM["t"], 1)
        pl.emit("tensor", f_tr)
        t_tr = pl.inc("t")
        last_kq_read_t[tc0 % 3] = t_tr
        pl.wait("vector", "t", t_tr)
        def f_tev(e, j0=j0, tc0=tc0):
            par = tc0 % 2
            if j0 == 0:
                dst = kT_g[:].rearrange("p (h t) -> p h t", t=2048)[:, :, tc0 * 128:tc0 * 128 + 128]
            else:
                dst = qT_g[:].rearrange("p (h t) -> p h t", t=TOK)[:, :, tc0 * 128:tc0 * 128 + 128]
            srcp = P[4 + par][:].rearrange("p (h t) -> p h t", t=128)
            e.tensor_copy(dst, srcp).then_inc(SM["v"], 1)
        pl.emit("vector", f_tev)
        last_tr_evac[par2] = ("v", pl.inc("v"))

    def flush_epi():
        hh0, v_rec0, _exp15 = pending_epi.pop()
        pl.wait("tensor", "v", v_rec0)
        def f_rb(e):
            e.matmul(P[6][:], lhsT=nurow[:], rhs=rec[:, 0:512], start=True, stop=True)
            last = e.matmul(P[7][:], lhsT=nurow[:], rhs=rec[:, 512:TOK], start=True, stop=True)
            last.then_inc(SM["t"], 1)
        pl.emit("tensor", f_rb)
        t_rb = pl.inc("t")
        pending_rb_t[0] = t_rb
        pl.wait("scalar", "t", t_rb)
        def f_epc(e):
            e.drain()
            e.activation(recb_w[:], PQ[3][:], AF.Copy).then_inc(SM["s"], 1)
        pl.emit("scalar", f_epc)
        s_epc = pl.inc("s")
        pl.wait("vector", "s", s_epc)
        def f_ep(e, hh0=hh0):
            e.drain()
            e.tensor_mul(attn_bf[:, hh0 * TOK:(hh0 + 1) * TOK], PQ[2][:], recb_w[:]).then_inc(SM["v"], 1)
        pl.emit("vector", f_ep)
        attn_done[("epi", hh0)] = pl.inc("v")

    def spans(kt):
        a, b = qr_of(kt)
        out = []
        if a < 512:
            out.append((0, a, min(b, 512)))
        if b > 512:
            out.append((1, max(a, 512), b))
        return out

    def emit_proj_tc(j, jw, tc, kq_evac_v):
        pp = P[tc % 4]
        def f_mm(e, jw=jw, tc=tc, pp=pp):
            slot = wt_sb[jw % 2]
            for ic in range(NIC):
                last = e.matmul(pp[:], lhsT=xn_chunk(ic, tc), rhs=slot[:, ic * 512:(ic + 1) * 512],
                         start=(ic == 0), stop=(ic == NIC - 1))
            last.then_inc(SM["t"], 1)
        pl.emit("tensor", f_mm)
        t_pj = pl.inc("t")

        pl.wait("scalar", "t", t_pj)
        def f_sq(e, tc=tc, pp=pp):
            p4 = tc % 4
            e.drain()
            for h in range(4):
                e.activation(sq_junk[:, TOK + h * 128: TOK + (h + 1) * 128], pp[:, h * 128:(h + 1) * 128],
                             AF.Square, accum_out=ssq4[p4][:, h:h + 1])
            e.drain()
            e.activation(std4[p4][:], ssq4[p4][:], AF.Sqrt).then_inc(SM["s"], 1)
        pl.emit("scalar", f_sq)
        s_sq = pl.inc("s")

        pl.wait("vector", "s", s_sq)
        if j != 1 and last_kq_read_t[tc % 3]:
            pl.wait("vector", "t", last_kq_read_t[tc % 3])
        def f_ev(e, j=j, tc=tc, pp=pp):
            par = tc % 2
            p4 = tc % 4
            e.drain()
            e.tensor_scalar(rec4[p4][:], std4[p4][:], float(EPS_L2), None, ALU.add)
            e.drain()
            e.reciprocal(rec4[p4][:], rec4[p4][:])
            e.drain()
            if j == 2:
                e.tensor_scalar_mul(rec4[p4][:], rec4[p4][:], tau_col[:])
                e.drain()
            for h in range(4):
                sc = rec4[p4][:, h:h + 1]
                if j == 1:
                    dst = v_g[:, tc * 512 + h * 128: tc * 512 + (h + 1) * 128]
                else:
                    dst = kq_tok[tc % 3][:, h * 128:(h + 1) * 128]
                last = e.tensor_scalar_mul(dst, pp[:, h * 128:(h + 1) * 128], sc)
            last.then_inc(SM["v"], 1)
        pl.emit("vector", f_ev)
        v_ev = pl.inc("v")
        kq_evac_v[(j, tc)] = (v_ev, s_sq)

        if j != 1:
            pending_tr.append((j, tc, v_ev))
            if len(pending_tr) > 2:
                emit_kq_transpose()

    emit_wload(0, 0)
    g0_kq = {}
    hook_state = {"first": True}

    def g0_hook(i, gate_a, gate_b):
        if hook_state["first"]:
            sl, dwv = wt_dw[0]
            pl.wait("tensor", sl, dwv)
            hook_state["first"] = False
        pl.wait("tensor", *gate_a)
        pl.wait("tensor", *gate_b)
        if i >= 4:
            pl.wait("tensor", "v", g0_kq[(0, i - 4)][0])
            pl.wait("tensor", "s", g0_kq[(0, i - 4)][1])
        emit_proj_tc(0, 0, i, g0_kq)

    phaseA(tile_hook=g0_hook)
    while pending_tr:
        emit_kq_transpose()
    proj_done_t[0] = pl.cnt["t"]
    emit_wload(0, 1)

    for g in range(NGRP):
        if pending_epi:
            flush_epi()
        kq_evac_v = g0_kq if g == 0 else {}
        ntc = {0: 16, 1: 16, 2: 8}
        for j in range(3):
            jw = 3 * g + j
            if g == 0 and j == 0:
                continue
            if jw + 1 < 3 * NGRP:
                emit_wload((jw + 1) // 3, (jw + 1) % 3)
            sl, dwv = wt_dw[jw]
            pl.wait("tensor", sl, dwv)
            for tc in range(ntc[j]):
                if tc >= 4:
                    pl.wait("tensor", "v", kq_evac_v[(j, tc - 4)][0])
                    pl.wait("tensor", "s", kq_evac_v[(j, tc - 4)][1])
                elif j > 0:
                    lastc = ntc[j - 1]
                    prev = (j - 1, lastc - 4 + tc)
                    pl.wait("tensor", "v", kq_evac_v[prev][0])
                    pl.wait("tensor", "s", kq_evac_v[prev][1])
                elif g > 0:
                    pl.wait("tensor", "v", attn_done[("epi", (g - 1) * 4 + 3)])
                emit_proj_tc(j, jw, tc, kq_evac_v)
            while pending_tr:
                emit_kq_transpose()
            proj_done_t[jw] = pl.cnt["t"]

        grp_ready_v = pl.cnt["v"]

        for h in range(4):
            hh = g * 4 + h
            exp_s = {}
            mask_v = {}
            av_t = {}
            sT_t = {}
            diag_v = {}

            def emit_av(kt, h=h, hh=hh):
                if kt >= 8:
                    pl.wait("tensor", "v", mask_v[kt])
                else:
                    pl.wait("tensor", "s", exp_s[kt])
                if kt == 0 and (h > 0 or g > 0):
                    pl.wait("tensor", "v", attn_done[("epi", hh - 1)])
                def f_av(e, h=h, kt=kt):
                    pT = pT_slots[kt % 4]
                    st_ = kt == 0
                    for (bk, a, b2) in spans(kt):
                        sp_ = kt == (11 if bk == 0 else NIC - 1)
                        e.matmul(P[4 + bk][:, a - bk * 512:b2 - bk * 512],
                                 lhsT=v_g[:, kt * 512 + h * 128: kt * 512 + (h + 1) * 128],
                                 rhs=pT[:, a:b2], start=st_, stop=sp_)
                        last = e.matmul(P[6 + bk][0:1, a - bk * 512:b2 - bk * 512],
                                 lhsT=onescol[:], rhs=pT[:, a:b2], start=st_, stop=sp_)
                    last.then_inc(SM["t"], 1)
                pl.emit("tensor", f_av)
                av_t[kt] = pl.inc("t")

            for kt in range(NIC):
                pair = [P[0], P[1]] if kt % 2 == 0 else [P[2], P[3]]
                if kt >= 2:
                    pl.wait("tensor", "s", exp_s[kt - 2])
                    if (kt - 2) in diag_v:
                        pl.wait("tensor", "v", diag_v[kt - 2])
                if kt == 1 and (g > 0 or h > 0):
                    pl.wait("tensor", "s", attn_done.get(("exp15", hh - 1), 0))
                if kt == 0:
                    pl.wait("tensor", "v", grp_ready_v)
                    if h > 0 or g > 0:
                        pl.wait("tensor", "s", attn_done.get(("exp14", hh - 1), 0))
                def f_sT(e, h=h, kt=kt, pair=pair):
                    for (bk, a, b2) in spans(kt):
                        last = e.matmul(pair[bk][:, a - bk * 512:b2 - bk * 512],
                                 lhsT=kT_g[:, h * 2048 + kt * 128: h * 2048 + (kt + 1) * 128],
                                 rhs=qT_g[:, h * TOK + a: h * TOK + b2],
                                 start=True, stop=True)
                    last.then_inc(SM["t"], 1)
                pl.emit("tensor", f_sT)
                sT_t[kt] = pl.inc("t")

                if kt < 8:
                    pl.wait("vector", "t", sT_t[kt])
                    def f_dg(e, kt=kt, pair=pair):
                        bk = kt // 4
                        a = (kt * 128) % 512
                        e.tensor_add(pair[bk][:, a:a + 128], pair[bk][:, a:a + 128], tri_sb[:]).then_inc(SM["v"], 1)
                    pl.emit("vector", f_dg)
                    diag_v[kt] = pl.inc("v")

                if kt < 8:
                    pl.wait("scalar", "v", diag_v[kt])
                else:
                    pl.wait("scalar", "t", sT_t[kt])
                if kt >= 4:
                    pl.wait("scalar", "t", av_t[kt - 4])
                    if (kt - 4) in mask_v:
                        pl.wait("scalar", "v", mask_v[kt - 4])
                def f_ex(e, kt=kt):
                    pT = pT_slots[kt % 4]
                    pq = PQ[0] if kt % 2 == 0 else PQ[1]
                    a, b2 = qr_of(kt)
                    e.drain()
                    e.activation(pT[:, a:b2], pq[:][:, a:b2], AF.Exp,
                                 bias=negtau_col[:]).then_inc(SM["s"], 1)
                pl.emit("scalar", f_ex)
                exp_s[kt] = pl.inc("s")

                if kt >= 8:
                    pl.wait("vector", "s", exp_s[kt])
                    def f_mk(e, kt=kt):
                        pT = pT_slots[kt % 4]
                        e.drain()
                        if kt < 12:
                            e.tensor_mul(pT[:, 0:TOK], pT[:, 0:TOK], m8_sb[:]).then_inc(SM["v"], 1)
                        else:
                            e.tensor_mul(pT[:, 512:TOK], pT[:, 512:TOK], m12_sb[:]).then_inc(SM["v"], 1)
                    pl.emit("vector", f_mk)
                    mask_v[kt] = pl.inc("v")

                if kt == 2 and pending_epi:
                    flush_epi()
                if kt >= 2:
                    emit_av(kt - 2)
            emit_av(NIC - 2)
            emit_av(NIC - 1)

            attn_done[("exp14", hh)] = exp_s[NIC - 2]
            attn_done[("exp15", hh)] = exp_s[NIC - 1]
            pl.wait("vector", "t", av_t[NIC - 1])
            if pending_rb_t[0]:
                pl.wait("vector", "t", pending_rb_t[0])
            def f_rc(e):
                e.drain()
                e.reciprocal(rec[:], PQ[3][0:1, :]).then_inc(SM["v"], 1)
            pl.emit("vector", f_rc)
            v_rec = pl.inc("v")
            pending_epi.append((hh, v_rec, exp_s[NIC - 1]))
    flush_epi()

    attn_all_v = attn_done[("epi", 15)]
    attn_all_t = pl.cnt["t"]

    wtr_v = {}
    pl.wait("tensor", "dg", pl.cnt["dg"])
    for tc in range(8):
        par = tc % 2
        pp = P[0] if par == 0 else P[2]
        if tc >= 2:
            pl.wait("tensor", "s", wtr_v[("exp_s", tc - 2)])
        else:
            pl.wait("tensor", "v", attn_all_v)
        def f_mm(e, tc=tc, pp=pp):
            for ic in range(NIC):
                last = e.matmul(pp[:, 0:NEXP], lhsT=xn_chunk(ic, tc),
                         rhs=gwt_sb[:, ic * NEXP:(ic + 1) * NEXP],
                         start=(ic == 0), stop=(ic == NIC - 1))
            last.then_inc(SM["t"], 1)
        pl.emit("tensor", f_mm)
        t_rt = pl.inc("t")

        pl.wait("vector", "t", t_rt)
        def f_nm(e, par=par, pp=pp):
            e.drain()
            e.reduce_max(negmax[par][:], pp[:, 0:NEXP], axis=AX.X, negate=True).then_inc(SM["v"], 1)
        pl.emit("vector", f_nm)
        v_nm = pl.inc("v")

        pl.wait("scalar", "v", v_nm)
        if tc >= 2:
            pl.wait("scalar", "v", wtr_v[("w_v", tc - 2)])
        def f_ex(e, par=par, pp=pp):
            e.drain()
            e.activation(probs_t[par][:], pp[:, 0:NEXP], AF.Exp, bias=negmax[par][:]).then_inc(SM["s"], 1)
        pl.emit("scalar", f_ex)
        s_ex = pl.inc("s")
        wtr_v[("exp_s", tc)] = s_ex

        pl.wait("vector", "s", s_ex)
        if tc >= 2:
            pl.wait("vector", "t", wtr_v[("t_wt", tc - 2)])
        def f_w(e, par=par):
            e.drain()
            e.max(out=top8[:], in_=probs_t[par][:])
            e.drain()
            e.tensor_copy(theta[:], top8[:, 7:8])
            e.drain()
            e.tensor_scalar(w_t[par][:], probs_t[par][:], theta[:], None, ALU.is_ge)
            e.drain()
            e.tensor_mul(w_t[par][:], w_t[par][:], probs_t[par][:])
            e.drain()
            e.reduce_sum(dnm[:], w_t[par][:], axis=AX.X)
            e.drain()
            e.reciprocal(rdn[:], dnm[:])
            e.drain()
            e.tensor_scalar_mul(w_t[par][:], w_t[par][:], rdn[:]).then_inc(SM["v"], 1)
        pl.emit("vector", f_w)
        v_w = pl.inc("v")
        wtr_v[("w_v", tc)] = v_w

        pl.wait("tensor", "v", v_w)
        if tc >= 2:
            pl.wait("tensor", "v", wtr_v[("wT_v", tc - 2)])
        def f_tr(e, par=par):
            dstp = P[1] if par == 0 else P[3]
            last = e.transpose(dstp[0:NEXP, 0:128], w_t[par][:], ident_f32[:])
            last.then_inc(SM["t"], 1)
        pl.emit("tensor", f_tr)
        t_wt = pl.inc("t")
        wtr_v[("t_wt", tc)] = t_wt
        pl.wait("vector", "t", t_wt)
        def f_we(e, tc=tc, par=par):
            srcp = P[1] if par == 0 else P[3]
            e.tensor_copy(wT_sb[:, tc * 128:(tc + 1) * 128], srcp[0:NEXP, 0:128]).then_inc(SM["v"], 1)
        pl.emit("vector", f_we)
        wtr_v[("wT_v", tc)] = pl.inc("v")
    router_done_v = wtr_v[("wT_v", 7)]

    def f_init(e):
        for hc in range(NIC):
            last = e.tensor_copy(acc[:, hc * TOK:(hc + 1) * TOK], attn_bf[:, hc * TOK:(hc + 1) * TOK])
        last.then_inc(SM["v"], 1)
    pl.emit("vector", f_init)
    acc_init_v = pl.inc("v")

    guw_done_t = {}
    gu_evac_v = {}
    silu_s = {}
    y_evac_v = {}
    wb_evac_v = {}
    wb_mm_t = {}
    y_done_t = {}
    guw_dw = {}
    wd_dw = {}
    wr_dw = {}

    t0_moe = pl.cnt["t"]
    pre_guw_t = {gidx: t0_moe + (gidx // NICC) * 23 + 2 + (gidx % NICC) for gidx in range(NEXP * NICC)}
    pre_y_t = {ee: t0_moe + 23 * (ee + 1) for ee in range(NEXP)}
    pre_wb_t = {ee: t0_moe + ee * 23 + 1 for ee in range(NEXP)}

    def stage_wr(ee):
        sl = f"wr{ee % 2}"
        if ee == 0:
            pl.wait("gpsimd", "v", router_done_v)
        if ee >= 2:
            pl.wait("gpsimd", "t", pre_wb_t[ee - 2])
        def f(e, ee=ee, sl=sl):
            e.dma_start(out=wr_sb[ee % 2][:], in_=wT_sb[ee:ee + 1, :]).then_inc(SM[sl], 16)
        pl.emit("gpsimd", f)
        wr_dw[ee] = (sl, pl.inc(sl, 16))

    stage_wr(0)
    stage_wr(1)
    for ee in range(NEXP):
        for ic in range(NICC):
            gidx = ee * NICC + ic
            slot = gidx % 2
            sl = f"gw{slot}"
            if gidx == 0:
                pl.wait("gpsimd", "v", attn_all_v)
                pl.wait("gpsimd", "t", attn_all_t)
            if gidx >= 2:
                pl.wait("gpsimd", "t", pre_guw_t[gidx - 2])
            def f(e, ee=ee, ic=ic, slot=slot, sl=sl):
                e.dma_start(out=guw_sb[slot][:], in_=guw_h[ee, ic, :, :]).then_inc(SM[sl], 16)
            pl.emit("gpsimd", f)
            guw_dw[gidx] = (sl, pl.inc(sl, 16))
        if ee + 2 < NEXP:
            stage_wr(ee + 2)
        if ee == 0:
            pl.wait("gpsimd", "v", acc_init_v)
        else:
            pl.wait("gpsimd", "t", pre_y_t[ee - 1])
        for ic in range(NICC):
            def f(e, ee=ee, ic=ic):
                e.dma_start(out=wd_sb[:, ic * C:(ic + 1) * C],
                            in_=wd_h[ee, ic * 128:(ic + 1) * 128, :]).then_inc(SM["dwd"], 16)
            pl.emit("gpsimd", f)
            pl.inc("dwd", 16)
        wd_dw[ee] = ("dwd", pl.cnt["dwd"])

    PG = [[P[0], P[1]], [P[4], P[5]]]
    PU = [[P[2], P[3]], [P[6], P[7]]]
    PY = [[P[0], P[1]], [P[2], P[3]]]
    PWB = [P[4], P[5]]

    pl.wait("tensor", "v", attn_all_v)
    for ee in range(NEXP):
        par = ee % 2
        sl, dwv = wr_dw[ee]
        pl.wait("tensor", sl, dwv)
        if ee >= 1:
            pl.wait("tensor", "v", gu_evac_v[(ee - 1) * NICC + NICC - 1])
            pl.wait("tensor", "s", silu_s[(ee - 1) * NICC + NICC - 1])
        def f_wb(e, ee=ee):
            e.matmul(PWB[0][:], lhsT=onesc_bf[:], rhs=wr_sb[ee % 2][:, 0:512], start=True, stop=True)
            last = e.matmul(PWB[1][:], lhsT=onesc_bf[:], rhs=wr_sb[ee % 2][:, 512:TOK], start=True, stop=True)
            last.then_inc(SM["t"], 1)
        pl.emit("tensor", f_wb)
        wb_mm_t[ee] = pl.inc("t")
        assert wb_mm_t[ee] == pre_wb_t[ee]

        pl.wait("vector", "t", wb_mm_t[ee])
        if ee >= 1:
            pass
        def f_wbe(e, par=par):
            e.drain()
            e.tensor_copy(wb_sb[par][:], PQ[2][:]).then_inc(SM["v"], 1)
        pl.emit("vector", f_wbe)
        wb_evac_v[ee] = pl.inc("v")

        for ic in range(NICC):
            gidx = ee * NICC + ic
            sl, dwv = guw_dw[gidx]
            pl.wait("tensor", sl, dwv)
            if ic >= 2:
                pl.wait("tensor", "v", gu_evac_v[gidx - 2])
                pl.wait("tensor", "s", silu_s[gidx - 2])
            if ic == 0 and ee >= 1:
                pl.wait("tensor", "v", y_evac_v[(ee - 1, NIC - 1)])
            if ic == 1:
                pl.wait("tensor", "v", wb_evac_v[ee])
            def f_gu(e, gidx=gidx):
                p2 = gidx % 2
                gp0, gp1 = PG[p2]
                up0, up1 = PU[p2]
                sg = guw_sb[gidx % 2]
                for icc in range(NIC):
                    base = icc * 256
                    lg = sg[:, base:base + 128]
                    lu = sg[:, base + 128:base + 256]
                    st_ = icc == 0
                    sp_ = icc == NIC - 1
                    rx0 = xn_own[:, icc * TOK: icc * TOK + 512]
                    rx1 = xn_own[:, icc * TOK + 512: (icc + 1) * TOK]
                    e.matmul(gp0[:], lhsT=lg, rhs=rx0, start=st_, stop=sp_)
                    e.matmul(gp1[:], lhsT=lg, rhs=rx1, start=st_, stop=sp_)
                    e.matmul(up0[:], lhsT=lu, rhs=rx0, start=st_, stop=sp_)
                    last = e.matmul(up1[:], lhsT=lu, rhs=rx1, start=st_, stop=sp_)
                last.then_inc(SM["t"], 1)
            pl.emit("tensor", f_gu)
            guw_done_t[gidx] = pl.inc("t")
            assert guw_done_t[gidx] == pre_guw_t[gidx]

            pl.wait("scalar", "t", guw_done_t[gidx])
            if gidx >= 1:
                pl.wait("scalar", "v", gu_evac_v[gidx - 1])
            def f_sil(e, gidx=gidx):
                e.drain()
                e.activation(gact[:], (PQ[0] if gidx % 2 == 0 else PQ[2])[:], AF.Silu).then_inc(SM["s"], 1)
            pl.emit("scalar", f_sil)
            silu_s[gidx] = pl.inc("s")

            pl.wait("vector", "s", silu_s[gidx])
            def f_gue(e, gidx=gidx, ee=ee, ic=ic, par=par):
                e.drain()
                e.tensor_mul(gact[:], gact[:], (PQ[1] if gidx % 2 == 0 else PQ[3])[:])
                e.drain()
                e.tensor_mul(gu_sb[par][:, ic * TOK:(ic + 1) * TOK], gact[:], wb_sb[par][:]).then_inc(SM["v"], 1)
            pl.emit("vector", f_gue)
            gu_evac_v[gidx] = pl.inc("v")

        sl, dwv = wd_dw[ee]
        pl.wait("tensor", sl, dwv)
        for hc in range(NIC):
            if hc >= 2:
                pl.wait("tensor", "v", y_evac_v[(ee, hc - 2)])
            if hc == 0:
                for ic in range(NICC):
                    if ic == 0:
                        pl.wait("tensor", "v", gu_evac_v[ee * NICC + 4])
                        pl.wait("tensor", "s", silu_s[ee * NICC + 4])
                    elif ic == 5:
                        pl.wait("tensor", "v", gu_evac_v[ee * NICC + 5])
                    def f_y1(e, ee=ee, ic=ic, par=par):
                        py0, py1 = PY[0]
                        lw = wd_sb[:, ic * C: ic * C + 128]
                        st_ = ic == 0
                        sp_ = ic == NICC - 1
                        e.matmul(py0[:], lhsT=lw, rhs=gu_sb[par][:, ic * TOK: ic * TOK + 512], start=st_, stop=sp_)
                        last = e.matmul(py1[:], lhsT=lw, rhs=gu_sb[par][:, ic * TOK + 512: (ic + 1) * TOK], start=st_, stop=sp_)
                        if sp_:
                            last.then_inc(SM["t"], 1)
                    pl.emit("tensor", f_y1)
                t_y = pl.inc("t")
                pl.wait("vector", "t", t_y)
                def f_ye0(e, ee=ee):
                    e.drain()
                    e.tensor_add(acc[:, 0:TOK], acc[:, 0:TOK], PQ[0][:]).then_inc(SM["v"], 1)
                pl.emit("vector", f_ye0)
                y_evac_v[(ee, 0)] = pl.inc("v")
                continue
            def f_y(e, ee=ee, hc=hc, par=par):
                py0, py1 = PY[hc % 2]
                for ic in range(NICC):
                    lw = wd_sb[:, ic * C + hc * 128: ic * C + (hc + 1) * 128]
                    st_ = ic == 0
                    sp_ = ic == NICC - 1
                    e.matmul(py0[:], lhsT=lw, rhs=gu_sb[par][:, ic * TOK: ic * TOK + 512], start=st_, stop=sp_)
                    last = e.matmul(py1[:], lhsT=lw, rhs=gu_sb[par][:, ic * TOK + 512: (ic + 1) * TOK], start=st_, stop=sp_)
                last.then_inc(SM["t"], 1)
            pl.emit("tensor", f_y)
            t_y = pl.inc("t")
            if hc == NIC - 1:
                y_done_t[ee] = t_y
                assert t_y == pre_y_t[ee]

            pl.wait("vector", "t", t_y)
            def f_ye(e, ee=ee, hc=hc):
                pq = PQ[hc % 2]
                dst = acc[:, hc * TOK:(hc + 1) * TOK]
                e.drain()
                e.tensor_add(dst, dst, pq[:]).then_inc(SM["v"], 1)
            pl.emit("vector", f_ye)
            y_evac_v[(ee, hc)] = pl.inc("v")

    moe_done_v = y_evac_v[(NEXP - 1, NIC - 1)]
    moe_done_t = pl.cnt["t"]

    st_count = 0
    ev_v = {}
    for tc in range(8):
        par = tc % 2
        sl = f"dx{par}"
        if tc < 2:
            pl.wait("sync", "v", moe_done_v)
        else:
            pl.wait("sync", "v", ev_v[tc - 2])
        def f_dx(e, tc=tc, par=par, sl=sl):
            e.dma_start(out=x2_sb[par][:], in_=x_h[tc * 128:(tc + 1) * 128, :]).then_inc(SM[sl], 16)
        pl.emit("sync", f_dx)
        x_dw2 = pl.inc(sl, 16)

        if tc == 0:
            pl.wait("tensor", "v", moe_done_v)
        if tc >= 2:
            pl.wait("tensor", "v", ev_v[tc - 2])
        def f_tr(e, tc=tc, par=par):
            base = 4 * par
            for hc in range(NIC):
                pb = P[base + hc // 4]
                col = (hc % 4) * 128
                last = e.transpose(pb[:, col:col + 128],
                            acc[:, hc * TOK + tc * 128: hc * TOK + tc * 128 + 128], ident_f32[:])
            last.then_inc(SM["t"], 1)
        pl.emit("tensor", f_tr)
        t_tr = pl.inc("t")

        pl.wait("vector", "t", t_tr)
        pl.wait("vector", sl, x_dw2)
        if tc >= 2:
            pl.wait("vector", "st", 16 * tc)
        def f_add(e, tc=tc, par=par):
            base = 4 * par
            e.drain()
            for j in range(4):
                last = e.tensor_add(out_sb[par][:, j * 512:(j + 1) * 512],
                             x2_sb[par][:, j * 512:(j + 1) * 512], P[base + j][:])
            last.then_inc(SM["v"], 1)
        pl.emit("vector", f_add)
        ev_v[tc] = pl.inc("v")

        pl.wait("sync", "v", ev_v[tc])
        def f_st(e, tc=tc, par=par):
            e.dma_start(out=out_h[tc * 128:(tc + 1) * 128, :], in_=out_sb[par][:]).then_inc(SM["st"], 16)
        pl.emit("sync", f_st)
        st_count = pl.inc("st", 16)
        if tc >= 1:
            pass
    pl.wait("sync", "st", st_count)

    with nc.Block() as block:
        @block.sync
        def _(eng):
            for f in pl.q["sync"]:
                f(eng)

        @block.gpsimd
        def _(eng):
            for f in pl.q["gpsimd"]:
                f(eng)

        @block.tensor
        def _(eng):
            for f in pl.q["tensor"]:
                f(eng)

        @block.scalar
        def _(eng):
            for f in pl.q["scalar"]:
                f(eng)

        @block.vector
        def _(eng):
            for f in pl.q["vector"]:
                f(eng)

    for g in pl._sem_guards:
        g.__exit__(None, None, None)
    return nc


_PERM = {}
def _perms():
    if _PERM:
        return _PERM
    q = [np.arange(i * 512, (i + 1) * 512) for i in range(4)]
    _PERM[0] = np.concatenate([q[0], q[3], q[1], q[2]])
    _PERM[1] = np.concatenate([q[1], q[2], q[0], q[3]])
    return _PERM


_CACHE = {}
def kernel(L0_attn_x, L0_attn_wq, L0_attn_wk, L0_attn_wv, L0_attn_tau, L0_attn_nu,
           L0_ffn_gate_w, L0_ffn_w_gate, L0_ffn_w_up, L0_ffn_w_down):
    x = np.asarray(L0_attn_x, dtype=np.float32)
    wq = np.asarray(L0_attn_wq, dtype=np.float32)
    wk = np.asarray(L0_attn_wk, dtype=np.float32)
    wv = np.asarray(L0_attn_wv, dtype=np.float32)
    tau = np.asarray(L0_attn_tau, dtype=np.float32).reshape(1, 1)
    nu = np.asarray(L0_attn_nu, dtype=np.float32).reshape(1, 1)
    gw = np.asarray(L0_ffn_gate_w, dtype=np.float32)
    wg = np.asarray(L0_ffn_w_gate, dtype=np.float32)
    wu = np.asarray(L0_ffn_w_up, dtype=np.float32)
    wd = np.asarray(L0_ffn_w_down, dtype=np.float32)

    perms = _perms()
    wqt = np.ascontiguousarray(wq.T)
    wkt = np.ascontiguousarray(wk.T)
    wvt = np.ascontiguousarray(wv.T)
    gwt = np.ascontiguousarray(gw.T.reshape(NIC, 128, NEXP).transpose(1, 0, 2).reshape(128, NIC * NEXP))
    wgr = wg.reshape(NEXP, NIC, 128, NICC, 128)
    wur = wu.reshape(NEXP, NIC, 128, NICC, 128)
    guw = np.ascontiguousarray(
        np.stack([wgr, wur], axis=3).transpose(0, 4, 2, 1, 3, 5).reshape(NEXP, NICC, 128, NIC * 2 * 128))
    wdc = np.ascontiguousarray(wd)

    kk = np.arange(128)[:, None]
    qq = np.arange(128)[None, :]
    tri = np.where(kk <= qq, 0.0, NEG_BIG).astype(ml_dtypes.bfloat16)
    m8 = {}
    m12 = {}
    m8[0] = np.concatenate([np.zeros((128, 512)), np.ones((128, 512))], axis=1).astype(ml_dtypes.bfloat16)
    m8[1] = np.ones((128, 1024), dtype=ml_dtypes.bfloat16)
    m12[0] = np.ones((128, 512), dtype=ml_dtypes.bfloat16)
    m12[1] = np.zeros((128, 512), dtype=ml_dtypes.bfloat16)

    if "nc" not in _CACHE:
        _CACHE["nc"] = build_nc()
    nc = _CACHE["nc"]

    in_maps = []
    for c in range(8):
        b, par = c // 2, c % 2
        xc = np.ascontiguousarray(x[b][perms[par]])
        in_maps.append(dict(
            x=xc, wqt=wqt, wkt=wkt, wvt=wvt, gwt=gwt, tau=tau, nu=nu,
            guw=guw, wd=wdc, trimask=tri, mask8=m8[par], mask12=m12[par]))

    res = run_bass_kernel_spmd(nc, in_maps, core_ids=list(range(8)))

    out = np.empty((B, S, C), dtype=np.float32)
    for c in range(8):
        b, par = c // 2, c % 2
        out[b][perms[par][:TOK]] = res.results[c]["out"]
    return out


# version 13 (final)
# speedup vs baseline: 1.0064x; 1.0064x over previous
import numpy as np
import ml_dtypes
import concourse.bass as bass
import concourse.mybir as mybir
from concourse.bass_utils import run_bass_kernel_spmd

F32 = mybir.dt.float32
BF16 = mybir.dt.bfloat16
AF = mybir.ActivationFunctionType
ALU = mybir.AluOpType
AX = mybir.AxisListType

B, S, C = 4, 2048, 2048
NH, DH = 16, 128
NEXP, CI = 32, 768
TOK = 1024
NIC = C // 128
NICC = CI // 128
NGRP = 4
EPS_LN = 1e-5
EPS_L2 = 1e-8
NEG_BIG = -1e30

KB = 1024


class Plan:
    def __init__(self, nc, sem_names):
        self.nc = nc
        self.q = {k: [] for k in ("sync", "gpsimd", "tensor", "vector", "scalar")}
        self.sems = {}
        self.cnt = {}
        self._sem_guards = []
        for nm in sem_names:
            g = nc.semaphore(nm)
            h = g.__enter__()
            self._sem_guards.append(g)
            self.sems[nm] = h
            self.cnt[nm] = 0

    def emit(self, eng, fn):
        self.q[eng].append(fn)

    def wait(self, eng, sem_name, val):
        if val <= 0:
            return
        h = self.sems[sem_name]
        self.q[eng].append(lambda e: e.wait_ge(h, val))

    def inc(self, sem_name, n=1):
        self.cnt[sem_name] += n
        return self.cnt[sem_name]


def build_nc():
    nc = bass.Bass("TRN2", target_bir_lowering=False, debug=False)

    x_h = nc.dram_tensor("x", [2 * TOK, C], F32, kind="ExternalInput").ap()
    wqt_h = nc.dram_tensor("wqt", [C, C], F32, kind="ExternalInput").ap()
    wkt_h = nc.dram_tensor("wkt", [C, C], F32, kind="ExternalInput").ap()
    wvt_h = nc.dram_tensor("wvt", [C, C], F32, kind="ExternalInput").ap()
    gwt_h = nc.dram_tensor("gwt", [128, NIC * NEXP], F32, kind="ExternalInput").ap()
    tau_h = nc.dram_tensor("tau", [1, 1], F32, kind="ExternalInput").ap()
    nu_h = nc.dram_tensor("nu", [1, 1], F32, kind="ExternalInput").ap()
    guw_h = nc.dram_tensor("guw", [NEXP, NICC, 128, NIC * 2 * 128], F32, kind="ExternalInput").ap()
    wd_h = nc.dram_tensor("wd", [NEXP, CI, C], F32, kind="ExternalInput").ap()
    tri_h = nc.dram_tensor("trimask", [128, 128], BF16, kind="ExternalInput").ap()
    m8_h = nc.dram_tensor("mask8", [128, TOK], BF16, kind="ExternalInput").ap()
    m12_h = nc.dram_tensor("mask12", [128, 512], BF16, kind="ExternalInput").ap()
    out_h = nc.dram_tensor("out", [TOK, C], F32, kind="ExternalOutput").ap()

    A = nc.alloc_sbuf_tensor
    ident_bf = A("ident_bf", [128, 128], BF16)
    ident_f32 = A("ident_f32", [128, 128], F32)
    tri_sb = A("tri_sb", [128, 128], BF16)
    m8_sb = A("m8_sb", [128, TOK], BF16)
    m12_sb = A("m12_sb", [128, 512], BF16)
    onesc = A("onesc", [1, 128], F32)
    onesc_bf = A("onesc_bf", [1, 128], BF16)
    onescol = A("onescol", [128, 1], BF16)
    tau_sb = A("tau_sb", [1, 1], F32)
    nu_sb = A("nu_sb", [1, 1], F32)
    negat1 = A("negat1", [1, 1], F32)
    nurow = A("nurow", [1, 128], F32)
    tau_col = A("tau_col", [128, 1], F32)
    negtau_col = A("negtau_col", [128, 1], F32)
    eps_col = A("eps_col", [128, 1], F32)
    gwt_sb = A("gwt_sb", [128, NIC * NEXP], BF16)
    wT_sb = A("wT_sb", [NEXP, TOK], F32)
    nmr = [A(f"nmr{i}", [128, 1], F32) for i in range(2)]
    negm = [A(f"negm{i}", [128, 1], F32) for i in range(2)]
    msq = [A(f"msq{i}", [128, 1], F32) for i in range(2)]
    varb = [A(f"varb{i}", [128, 1], F32) for i in range(2)]
    stdb = [A(f"stdb{i}", [128, 1], F32) for i in range(2)]
    rcol = [A(f"rcol{i}", [128, 1], F32) for i in range(2)]
    negmr = [A(f"negmr{i}", [128, 1], F32) for i in range(2)]
    ssq_ln = [A(f"ssq_ln{i}", [128, 1], F32) for i in range(2)]
    ssq4 = [A(f"ssq4_{i}", [128, 4], F32) for i in range(4)]
    std4 = [A(f"std4_{i}", [128, 4], F32) for i in range(4)]
    rec4 = [A(f"rec4_{i}", [128, 4], F32) for i in range(4)]
    probs_t = [A(f"probs{i}", [128, NEXP], F32) for i in range(2)]
    w_t = [A(f"wt{i}", [128, NEXP], F32) for i in range(2)]
    top8 = A("top8", [128, 8], F32)
    theta = A("theta", [128, 1], F32)
    dnm = A("dnm", [128, 1], F32)
    rdn = A("rdn", [128, 1], F32)
    negmax = [A(f"negmax{i}", [128, 1], F32) for i in range(2)]
    rec = A("rec", [1, TOK], F32)
    kq_tok = [A(f"kq_tok{i}", [128, 512], F32) for i in range(3)]
    sq_junk = A("sq_junk", [128, C], BF16)

    auto_high = 229376 - nc.sbuf_bytes_remaining
    assert auto_high <= 44 * KB - 256, f"autos too big: high water {auto_high}"
    AT = nc.alloc_sbuf_tensor_at
    xn_own = AT("xn_own", [128, NIC * TOK], BF16, offset=44 * KB)
    attn_bf = AT("attn_bf", [128, NIC * TOK], BF16, offset=76 * KB)
    wd_sb = AT("wd_sb", [128, NICC * C], BF16, offset=76 * KB)
    x_sb = [AT(f"x_sb{i}", [128, C], F32, offset=(108 + 8 * i) * KB) for i in range(2)]
    scratch = AT("scratch", [128, C], BF16, offset=212 * KB)
    xn_oth = AT("xn_oth", [128, NIC * TOK], BF16, offset=124 * KB)
    wt_sb1 = AT("wt_sb1", [128, NIC * 512], BF16, offset=108 * KB)
    kT_g = AT("kT_g", [128, 4 * 2048], BF16, offset=156 * KB)
    v_g = AT("v_g", [128, NIC * 512], BF16, offset=172 * KB)
    qT_g = AT("qT_g", [128, 4 * TOK], BF16, offset=188 * KB)
    wt_sb0 = AT("wt_sb0", [128, NIC * 512], BF16, offset=196 * KB)
    recb_w = AT("recb_w", [128, TOK], F32, offset=216 * KB)
    wt_sb = [wt_sb0, wt_sb1]
    acc = AT("acc", [128, NIC * TOK], F32, offset=108 * KB)
    guw_sb = [AT(f"guw_sb{i}", [128, NIC * 2 * 128], BF16, offset=(172 + 8 * i) * KB) for i in range(2)]
    gu_sb = [AT(f"gu_sb{i}", [128, NICC * TOK], BF16, offset=(188 + 12 * i) * KB) for i in range(2)]
    gact = AT("gact", [128, TOK], BF16, offset=212 * KB)
    wb_sb = [AT(f"wb_sb{i}", [128, TOK], BF16, offset=(214 + 2 * i) * KB) for i in range(2)]
    wr_sb = [AT(f"wr_sb{i}", [1, TOK], BF16, offset=(218 + 2 * i) * KB) for i in range(2)]
    x2_sb = [AT(f"x2_sb{i}", [128, C], F32, offset=(172 + 8 * i) * KB) for i in range(2)]
    out_sb = [AT(f"out_sb{i}", [128, C], F32, offset=(188 + 8 * i) * KB) for i in range(2)]

    PQ = [nc.alloc_psum_tensor(f"PQ{i}", [128, 1024], F32) for i in range(4)]
    P = [PQ[i // 2][:, (i % 2) * 512:(i % 2) * 512 + 512] for i in range(8)]

    sem_names = (
        "t", "v", "s", "g",
        "dx0", "dx1",
        "dm",
        "dg",
        "wt0", "wt1",
        "gw0", "gw1",
        "dwd",
        "wr0", "wr1",
        "st",
    )
    pl = Plan(nc, sem_names)
    SM = pl.sems

    def setup():
        def f_sync(e):
            e.dma_start(out=tri_sb[:], in_=tri_h).then_inc(SM["dm"], 16)
            e.dma_start(out=m8_sb[:], in_=m8_h).then_inc(SM["dm"], 16)
            e.dma_start(out=m12_sb[:], in_=m12_h).then_inc(SM["dm"], 16)
            e.dma_start(out=tau_sb[:], in_=tau_h).then_inc(SM["dm"], 16)
            e.dma_start(out=nu_sb[:], in_=nu_h).then_inc(SM["dm"], 16)
        pl.emit("sync", f_sync)
        for _ in range(5):
            pl.inc("dm", 16)

        def f_g(e):
            e.memset(ident_bf[:], 0.0)
            e.drain()
            e.affine_select(out=ident_bf[:], in_=ident_bf[:], compare_op=ALU.not_equal,
                            fill=1.0, base=0, pattern=[[-1, 128]], channel_multiplier=1)
            e.memset(ident_f32[:], 0.0)
            e.drain()
            e.affine_select(out=ident_f32[:], in_=ident_f32[:], compare_op=ALU.not_equal,
                            fill=1.0, base=0, pattern=[[-1, 128]], channel_multiplier=1)
            e.memset(onesc[:], 1.0)
            e.memset(onesc_bf[:], 1.0)
            e.memset(onescol[:], 1.0)
            e.memset(eps_col[:], EPS_LN).then_inc(SM["g"], 1)
            e.dma_start(out=gwt_sb[:], in_=gwt_h).then_inc(SM["dg"], 16)
        pl.emit("gpsimd", f_g)
        pl.inc("g", 1)
        pl.inc("dg", 16)
        pl.wait("tensor", "g", 1)
        pl.wait("scalar", "g", 1)
        pl.wait("vector", "g", 1)

        pl.wait("vector", "dm", pl.cnt["dm"])
        def f_v(e):
            e.tensor_scalar_mul(negat1[:], tau_sb[:], -1.0)
            e.tensor_scalar_mul(nurow[:], onesc[:], nu_sb[:])
            e.drain()
            e.tensor_tensor(negat1[:], negat1[:], tau_sb[:], ALU.min).then_inc(SM["v"], 1)
        pl.emit("vector", f_v)
        v_tau = pl.inc("v")

        pl.wait("tensor", "v", v_tau)
        def f_t(e):
            e.matmul(P[6][:, 0:1], lhsT=onesc[:], rhs=tau_sb[:], start=True, stop=True)
            e.matmul(P[6][:, 1:2], lhsT=onesc[:], rhs=negat1[:], start=True, stop=True).then_inc(SM["t"], 1)
        pl.emit("tensor", f_t)
        t_bc = pl.inc("t")
        pl.wait("vector", "t", t_bc)
        def f_v2(e):
            e.tensor_copy(tau_col[:], P[6][:, 0:1])
            e.tensor_copy(negtau_col[:], P[6][:, 1:2]).then_inc(SM["v"], 1)
        pl.emit("vector", f_v2)
        return pl.inc("v")

    v_consts = setup()

    ln_state = {}
    last_tr_evac = {0: None, 1: None}
    def phaseA(tile_hook=None):
        tr_evac_v = {}
        norm_v = {}
        sq_s = {}
        x_dw = {}
        pendA = []

        def emit_A_tr(i):
            par = i % 2
            pl.wait("tensor", *norm_v[i])
            for gi in range(4):
                if last_tr_evac[gi % 2]:
                    pl.wait("tensor", *last_tr_evac[gi % 2])
                def f_tr(e, i=i, gi=gi, par=par):
                    pb = P[4 + gi % 2]
                    for j in range(4):
                        ic = gi * 4 + j
                        last = e.transpose(pb[:, j * 128:(j + 1) * 128],
                                           x_sb[par][:, ic * 128:(ic + 1) * 128], ident_f32[:])
                    last.then_inc(SM["t"], 1)
                pl.emit("tensor", f_tr)
                t_tr = pl.inc("t")
                if gi == 3:
                    ln_state[("tr_t", i)] = t_tr
                eng = "vector" if gi % 2 == 0 else "scalar"
                pl.wait(eng, "t", t_tr)
                def f_ev(e, i=i, gi=gi, eng=eng):
                    dstt = xn_own if i < 8 else xn_oth
                    tc = i % 8
                    dst = dstt[:].rearrange("p (ic t) -> p ic t", t=TOK)[:, gi * 4:(gi + 1) * 4, tc * 128:tc * 128 + 128]
                    srcp = P[4 + gi % 2][:].rearrange("p (ic t) -> p ic t", t=128)
                    if eng == "vector":
                        e.tensor_copy(dst, srcp).then_inc(SM["v"], 1)
                    else:
                        e.drain()
                        e.activation(dst, srcp, AF.Copy).then_inc(SM["s"], 1)
                pl.emit(eng, f_ev)
                if eng == "vector":
                    val = ("v", pl.inc("v"))
                else:
                    val = ("s", pl.inc("s"))
                tr_evac_v[i * 4 + gi] = val
                last_tr_evac[gi % 2] = val
        for i in range(16):
            par = i % 2
            sl = f"dx{par}"
            if i >= 2:
                pl.wait("sync", "t", ln_state[("tr_t", i - 2)])
                pl.wait("sync", "s", sq_s[i - 2])
            def f_dx(e, i=i, par=par, sl=sl):
                e.dma_start(out=x_sb[par][:], in_=x_h[i * 128:(i + 1) * 128, :]).then_inc(SM[sl], 16)
            pl.emit("sync", f_dx)
            x_dw[i] = pl.inc(sl, 16)

            pl.wait("scalar", sl, x_dw[i])
            def f_sq(e, i=i, par=par):
                e.drain()
                e.activation(sq_junk[:], x_sb[par][:], AF.Square, accum_out=ssq_ln[par][:])
                e.drain()
                e.activation(sq_junk[:], x_sb[par][:], AF.Copy, accum_out=nmr[par][:]).then_inc(SM["s"], 1)
            pl.emit("scalar", f_sq)
            sq_s[i] = pl.inc("s")

            pl.wait("vector", "s", sq_s[i])
            def f_st2(e, i=i, par=par):
                e.tensor_scalar_mul(negm[par][:], nmr[par][:], -1.0 / C)
                e.drain()
                e.tensor_mul(msq[par][:], negm[par][:], negm[par][:])
                e.drain()
                e.tensor_scalar(varb[par][:], ssq_ln[par][:], 1.0 / C, None, ALU.mult)
                e.drain()
                e.tensor_sub(varb[par][:], varb[par][:], msq[par][:]).then_inc(SM["v"], 1)
            pl.emit("vector", f_st2)
            v_var = pl.inc("v")

            pl.wait("scalar", "v", v_var)
            def f_sd(e, i=i, par=par):
                e.drain()
                e.activation(stdb[par][:], varb[par][:], AF.Sqrt, bias=eps_col[:]).then_inc(SM["s"], 1)
            pl.emit("scalar", f_sd)
            s_std = pl.inc("s")

            pl.wait("vector", "s", s_std)
            def f_nm(e, i=i, par=par):
                e.reciprocal(rcol[par][:], stdb[par][:])
                e.drain()
                e.tensor_mul(negmr[par][:], negm[par][:], rcol[par][:]).then_inc(SM["v"], 1)
            pl.emit("vector", f_nm)
            v_nmr = pl.inc("v")
            pl.wait("gpsimd", "v", v_nmr)
            def f_nm2(e, i=i, par=par):
                e.drain()
                e.tensor_scalar(x_sb[par][:], x_sb[par][:], rcol[par][:], negmr[par][:],
                                ALU.mult, ALU.add).then_inc(SM["g"], 1)
            pl.emit("gpsimd", f_nm2)
            norm_v[i] = ("g", pl.inc("g"))

            pendA.append(i)
            if len(pendA) > 1:
                done = pendA.pop(0)
                emit_A_tr(done)
                if tile_hook:
                    tile_hook(done, tr_evac_v[done * 4 + 2], tr_evac_v[done * 4 + 3])
        while pendA:
            done = pendA.pop(0)
            emit_A_tr(done)
            if tile_hook:
                tile_hook(done, tr_evac_v[done * 4 + 2], tr_evac_v[done * 4 + 3])
        ln_state["done_v"] = pl.cnt["v"]
        ln_state["done_s"] = pl.cnt["s"]
        ln_state["done_t"] = ln_state[("tr_t", 15)]
        ln_state["last_x_v"] = norm_v[15]
        ln_state["last_x_s"] = sq_s[15]
    W_SRC = [wkt_h, wvt_h, wqt_h]
    wt_dw = {}
    proj_done_t = {}
    attn_done = {}

    def emit_wload(g, j):
        jw = 3 * g + j
        slot = jw % 2
        sl = f"wt{slot}"
        src = W_SRC[j]
        if jw == 1:
            pl.wait("gpsimd", "t", ln_state["done_t"])
            pl.wait("gpsimd", "s", ln_state["last_x_s"])
        if jw >= 2:
            pl.wait("gpsimd", "t", proj_done_t[jw - 2])
        def f(e, g=g, src=src, slot=slot, sl=sl):
            for ic in range(NIC):
                e.dma_start(out=wt_sb[slot][:, ic * 512:(ic + 1) * 512],
                            in_=src[ic * 128:(ic + 1) * 128, g * 512:(g + 1) * 512]).then_inc(SM[sl], 16)
        pl.emit("gpsimd", f)
        for _ in range(NIC):
            pl.inc(sl, 16)
        wt_dw[jw] = (sl, pl.cnt[sl])

    def xn_chunk(ic, tc):
        t_ = xn_own if tc < 8 else xn_oth
        tcl = tc % 8
        return t_[:, ic * TOK + tcl * 128: ic * TOK + tcl * 128 + 128]


    qr_of = lambda kt: (kt * 128, TOK) if kt < 8 else ((0, TOK) if kt < 12 else (512, TOK))
    KT_SEQ = list(range(12)) + [14, 15, 12, 13]
    BANK_LAST = {}
    for _kt in KT_SEQ:
        a, b = qr_of(_kt)
        if a < 512:
            BANK_LAST[0] = _kt
        if b > 512:
            BANK_LAST[1] = _kt
    pT_slots = [scratch[:, 0:TOK], scratch[:, TOK:2 * TOK], sq_junk[:, 0:TOK], sq_junk[:, TOK:2 * TOK]]
    last_kq_read_t = {0: 0, 1: 0, 2: 0}
    pending_epi = []
    pending_rb_t = [0]
    pending_tr = []

    def emit_kq_transpose():
        j0, tc0, v_ev0 = pending_tr.pop(0)
        par2 = tc0 % 2
        if last_tr_evac[par2]:
            pl.wait("tensor", *last_tr_evac[par2])
        pl.wait("tensor", "v", v_ev0)
        def f_tr(e, tc0=tc0):
            par = tc0 % 2
            for h in range(4):
                last = e.transpose(P[4 + par][:, h * 128:(h + 1) * 128],
                                   kq_tok[tc0 % 3][:, h * 128:(h + 1) * 128], ident_f32[:])
            last.then_inc(SM["t"], 1)
        pl.emit("tensor", f_tr)
        t_tr = pl.inc("t")
        last_kq_read_t[tc0 % 3] = t_tr
        pl.wait("vector", "t", t_tr)
        def f_tev(e, j0=j0, tc0=tc0):
            par = tc0 % 2
            if j0 == 0:
                dst = kT_g[:].rearrange("p (h t) -> p h t", t=2048)[:, :, tc0 * 128:tc0 * 128 + 128]
            else:
                dst = qT_g[:].rearrange("p (h t) -> p h t", t=TOK)[:, :, tc0 * 128:tc0 * 128 + 128]
            srcp = P[4 + par][:].rearrange("p (h t) -> p h t", t=128)
            e.tensor_copy(dst, srcp).then_inc(SM["v"], 1)
        pl.emit("vector", f_tev)
        last_tr_evac[par2] = ("v", pl.inc("v"))

    def flush_epi():
        hh0, v_rec0, _exp15 = pending_epi.pop()
        pl.wait("tensor", "v", v_rec0)
        def f_rb(e):
            e.matmul(P[6][:], lhsT=nurow[:], rhs=rec[:, 0:512], start=True, stop=True)
            last = e.matmul(P[7][:], lhsT=nurow[:], rhs=rec[:, 512:TOK], start=True, stop=True)
            last.then_inc(SM["t"], 1)
        pl.emit("tensor", f_rb)
        t_rb = pl.inc("t")
        pending_rb_t[0] = t_rb
        pl.wait("scalar", "t", t_rb)
        def f_epc(e):
            e.drain()
            e.activation(recb_w[:], PQ[3][:], AF.Copy).then_inc(SM["s"], 1)
        pl.emit("scalar", f_epc)
        s_epc = pl.inc("s")
        pl.wait("vector", "s", s_epc)
        def f_ep(e, hh0=hh0):
            e.drain()
            e.tensor_mul(attn_bf[:, hh0 * TOK:(hh0 + 1) * TOK], PQ[2][:], recb_w[:]).then_inc(SM["v"], 1)
        pl.emit("vector", f_ep)
        attn_done[("epi", hh0)] = pl.inc("v")

    def spans(kt):
        a, b = qr_of(kt)
        out = []
        if a < 512:
            out.append((0, a, min(b, 512)))
        if b > 512:
            out.append((1, max(a, 512), b))
        return out

    def emit_proj_tc(j, jw, tc, kq_evac_v):
        pp = P[tc % 4]
        def f_mm(e, jw=jw, tc=tc, pp=pp):
            slot = wt_sb[jw % 2]
            for ic in range(NIC):
                last = e.matmul(pp[:], lhsT=xn_chunk(ic, tc), rhs=slot[:, ic * 512:(ic + 1) * 512],
                         start=(ic == 0), stop=(ic == NIC - 1))
            last.then_inc(SM["t"], 1)
        pl.emit("tensor", f_mm)
        t_pj = pl.inc("t")

        pl.wait("scalar", "t", t_pj)
        def f_sq(e, tc=tc, pp=pp):
            p4 = tc % 4
            e.drain()
            for h in range(4):
                e.activation(sq_junk[:, TOK + h * 128: TOK + (h + 1) * 128], pp[:, h * 128:(h + 1) * 128],
                             AF.Square, accum_out=ssq4[p4][:, h:h + 1])
            e.drain()
            e.activation(std4[p4][:], ssq4[p4][:], AF.Sqrt).then_inc(SM["s"], 1)
        pl.emit("scalar", f_sq)
        s_sq = pl.inc("s")

        pl.wait("vector", "s", s_sq)
        if j != 1 and last_kq_read_t[tc % 3]:
            pl.wait("vector", "t", last_kq_read_t[tc % 3])
        def f_ev(e, j=j, tc=tc, pp=pp):
            par = tc % 2
            p4 = tc % 4
            e.drain()
            e.tensor_scalar(rec4[p4][:], std4[p4][:], float(EPS_L2), None, ALU.add)
            e.drain()
            e.reciprocal(rec4[p4][:], rec4[p4][:])
            e.drain()
            if j == 2:
                e.tensor_scalar_mul(rec4[p4][:], rec4[p4][:], tau_col[:])
                e.drain()
            for h in range(4):
                sc = rec4[p4][:, h:h + 1]
                if j == 1:
                    dst = v_g[:, tc * 512 + h * 128: tc * 512 + (h + 1) * 128]
                else:
                    dst = kq_tok[tc % 3][:, h * 128:(h + 1) * 128]
                last = e.tensor_scalar_mul(dst, pp[:, h * 128:(h + 1) * 128], sc)
            last.then_inc(SM["v"], 1)
        pl.emit("vector", f_ev)
        v_ev = pl.inc("v")
        kq_evac_v[(j, tc)] = (v_ev, s_sq)

        if j != 1:
            pending_tr.append((j, tc, v_ev))
            if len(pending_tr) > 2:
                emit_kq_transpose()

    emit_wload(0, 0)
    g0_kq = {}
    hook_state = {"first": True}

    def g0_hook(i, gate_a, gate_b):
        if hook_state["first"]:
            sl, dwv = wt_dw[0]
            pl.wait("tensor", sl, dwv)
            hook_state["first"] = False
        pl.wait("tensor", *gate_a)
        pl.wait("tensor", *gate_b)
        if i >= 4:
            pl.wait("tensor", "v", g0_kq[(0, i - 4)][0])
            pl.wait("tensor", "s", g0_kq[(0, i - 4)][1])
        emit_proj_tc(0, 0, i, g0_kq)

    phaseA(tile_hook=g0_hook)
    while pending_tr:
        emit_kq_transpose()
    proj_done_t[0] = pl.cnt["t"]
    emit_wload(0, 1)

    for g in range(NGRP):
        if pending_epi:
            flush_epi()
        kq_evac_v = g0_kq if g == 0 else {}
        ntc = {0: 16, 1: 16, 2: 8}
        for j in range(3):
            jw = 3 * g + j
            if g == 0 and j == 0:
                continue
            if jw + 1 < 3 * NGRP:
                emit_wload((jw + 1) // 3, (jw + 1) % 3)
            sl, dwv = wt_dw[jw]
            pl.wait("tensor", sl, dwv)
            for tc in range(ntc[j]):
                if tc >= 4:
                    pl.wait("tensor", "v", kq_evac_v[(j, tc - 4)][0])
                    pl.wait("tensor", "s", kq_evac_v[(j, tc - 4)][1])
                elif j > 0:
                    lastc = ntc[j - 1]
                    prev = (j - 1, lastc - 4 + tc)
                    pl.wait("tensor", "v", kq_evac_v[prev][0])
                    pl.wait("tensor", "s", kq_evac_v[prev][1])
                elif g > 0:
                    pl.wait("tensor", "v", attn_done[("epi", (g - 1) * 4 + 3)])
                emit_proj_tc(j, jw, tc, kq_evac_v)
            while pending_tr:
                emit_kq_transpose()
            proj_done_t[jw] = pl.cnt["t"]

        grp_ready_v = pl.cnt["v"]

        for h in range(4):
            hh = g * 4 + h
            exp_s = {}
            mask_v = {}
            av_t = {}
            sT_t = {}
            diag_v = {}

            def emit_av(kt, pos, h=h, hh=hh):
                if kt >= 8:
                    pl.wait("tensor", "v", mask_v[kt])
                else:
                    pl.wait("tensor", "s", exp_s[kt])
                if pos == 0 and (h > 0 or g > 0):
                    pl.wait("tensor", "v", attn_done[("epi", hh - 1)])
                def f_av(e, h=h, kt=kt, pos=pos):
                    pT = pT_slots[pos % 4]
                    st_ = pos == 0
                    for (bk, a, b2) in spans(kt):
                        sp_ = kt == BANK_LAST[bk]
                        e.matmul(P[4 + bk][:, a - bk * 512:b2 - bk * 512],
                                 lhsT=v_g[:, kt * 512 + h * 128: kt * 512 + (h + 1) * 128],
                                 rhs=pT[:, a:b2], start=st_, stop=sp_)
                        last = e.matmul(P[6 + bk][0:1, a - bk * 512:b2 - bk * 512],
                                 lhsT=onescol[:], rhs=pT[:, a:b2], start=st_, stop=sp_)
                    last.then_inc(SM["t"], 1)
                pl.emit("tensor", f_av)
                av_t[kt] = pl.inc("t")

            for pos, kt in enumerate(KT_SEQ):
                pair = [P[0], P[1]] if pos % 2 == 0 else [P[2], P[3]]
                if pos >= 2:
                    pkt = KT_SEQ[pos - 2]
                    pl.wait("tensor", "s", exp_s[pkt])
                    if pkt in diag_v:
                        pl.wait("tensor", "v", diag_v[pkt])
                if pos == 1 and (g > 0 or h > 0):
                    pl.wait("tensor", "s", attn_done.get(("exp_pong", hh - 1), 0))
                if pos == 0:
                    pl.wait("tensor", "v", grp_ready_v)
                    if h > 0 or g > 0:
                        pl.wait("tensor", "s", attn_done.get(("exp_ping", hh - 1), 0))
                def f_sT(e, h=h, kt=kt, pair=pair):
                    for (bk, a, b2) in spans(kt):
                        last = e.matmul(pair[bk][:, a - bk * 512:b2 - bk * 512],
                                 lhsT=kT_g[:, h * 2048 + kt * 128: h * 2048 + (kt + 1) * 128],
                                 rhs=qT_g[:, h * TOK + a: h * TOK + b2],
                                 start=True, stop=True)
                    last.then_inc(SM["t"], 1)
                pl.emit("tensor", f_sT)
                sT_t[kt] = pl.inc("t")

                if kt < 8:
                    pl.wait("vector", "t", sT_t[kt])
                    def f_dg(e, kt=kt, pair=pair):
                        bk = kt // 4
                        a = (kt * 128) % 512
                        e.tensor_add(pair[bk][:, a:a + 128], pair[bk][:, a:a + 128], tri_sb[:]).then_inc(SM["v"], 1)
                    pl.emit("vector", f_dg)
                    diag_v[kt] = pl.inc("v")

                if kt < 8:
                    pl.wait("scalar", "v", diag_v[kt])
                else:
                    pl.wait("scalar", "t", sT_t[kt])
                if pos >= 4:
                    pkt4 = KT_SEQ[pos - 4]
                    pl.wait("scalar", "t", av_t[pkt4])
                    if pkt4 in mask_v:
                        pl.wait("scalar", "v", mask_v[pkt4])
                def f_ex(e, kt=kt, pos=pos):
                    pT = pT_slots[pos % 4]
                    pq = PQ[0] if pos % 2 == 0 else PQ[1]
                    a, b2 = qr_of(kt)
                    e.drain()
                    e.activation(pT[:, a:b2], pq[:][:, a:b2], AF.Exp,
                                 bias=negtau_col[:]).then_inc(SM["s"], 1)
                pl.emit("scalar", f_ex)
                exp_s[kt] = pl.inc("s")

                if kt >= 8:
                    pl.wait("vector", "s", exp_s[kt])
                    def f_mk(e, kt=kt, pos=pos):
                        pT = pT_slots[pos % 4]
                        e.drain()
                        if kt < 12:
                            e.tensor_mul(pT[:, 0:TOK], pT[:, 0:TOK], m8_sb[:]).then_inc(SM["v"], 1)
                        else:
                            e.tensor_mul(pT[:, 512:TOK], pT[:, 512:TOK], m12_sb[:]).then_inc(SM["v"], 1)
                    pl.emit("vector", f_mk)
                    mask_v[kt] = pl.inc("v")

                if pos == 2 and pending_epi:
                    flush_epi()
                if pos >= 2:
                    emit_av(KT_SEQ[pos - 2], KT_SEQ.index(KT_SEQ[pos - 2]))
            emit_av(KT_SEQ[NIC - 2], NIC - 2)
            emit_av(KT_SEQ[NIC - 1], NIC - 1)

            attn_done[("exp_ping", hh)] = exp_s[KT_SEQ[NIC - 2]]
            attn_done[("exp_pong", hh)] = exp_s[KT_SEQ[NIC - 1]]
            pl.wait("vector", "t", av_t[KT_SEQ[NIC - 1]])
            if pending_rb_t[0]:
                pl.wait("vector", "t", pending_rb_t[0])
            def f_rc(e):
                e.drain()
                e.reciprocal(rec[:], PQ[3][0:1, :]).then_inc(SM["v"], 1)
            pl.emit("vector", f_rc)
            v_rec = pl.inc("v")
            pending_epi.append((hh, v_rec, exp_s[NIC - 1]))
    flush_epi()

    attn_all_v = attn_done[("epi", 15)]
    attn_all_t = pl.cnt["t"]

    wtr_v = {}
    pl.wait("tensor", "dg", pl.cnt["dg"])
    for tc in range(8):
        par = tc % 2
        pp = P[0] if par == 0 else P[2]
        if tc >= 2:
            pl.wait("tensor", "s", wtr_v[("exp_s", tc - 2)])
        else:
            pl.wait("tensor", "v", attn_all_v)
        def f_mm(e, tc=tc, pp=pp):
            for ic in range(NIC):
                last = e.matmul(pp[:, 0:NEXP], lhsT=xn_chunk(ic, tc),
                         rhs=gwt_sb[:, ic * NEXP:(ic + 1) * NEXP],
                         start=(ic == 0), stop=(ic == NIC - 1))
            last.then_inc(SM["t"], 1)
        pl.emit("tensor", f_mm)
        t_rt = pl.inc("t")

        pl.wait("vector", "t", t_rt)
        def f_nm(e, par=par, pp=pp):
            e.drain()
            e.reduce_max(negmax[par][:], pp[:, 0:NEXP], axis=AX.X, negate=True).then_inc(SM["v"], 1)
        pl.emit("vector", f_nm)
        v_nm = pl.inc("v")

        pl.wait("scalar", "v", v_nm)
        if tc >= 2:
            pl.wait("scalar", "v", wtr_v[("w_v", tc - 2)])
        def f_ex(e, par=par, pp=pp):
            e.drain()
            e.activation(probs_t[par][:], pp[:, 0:NEXP], AF.Exp, bias=negmax[par][:]).then_inc(SM["s"], 1)
        pl.emit("scalar", f_ex)
        s_ex = pl.inc("s")
        wtr_v[("exp_s", tc)] = s_ex

        pl.wait("vector", "s", s_ex)
        if tc >= 2:
            pl.wait("vector", "t", wtr_v[("t_wt", tc - 2)])
        def f_w(e, par=par):
            e.drain()
            e.max(out=top8[:], in_=probs_t[par][:])
            e.drain()
            e.tensor_copy(theta[:], top8[:, 7:8])
            e.drain()
            e.tensor_scalar(w_t[par][:], probs_t[par][:], theta[:], None, ALU.is_ge)
            e.drain()
            e.tensor_mul(w_t[par][:], w_t[par][:], probs_t[par][:])
            e.drain()
            e.reduce_sum(dnm[:], w_t[par][:], axis=AX.X)
            e.drain()
            e.reciprocal(rdn[:], dnm[:])
            e.drain()
            e.tensor_scalar_mul(w_t[par][:], w_t[par][:], rdn[:]).then_inc(SM["v"], 1)
        pl.emit("vector", f_w)
        v_w = pl.inc("v")
        wtr_v[("w_v", tc)] = v_w

        pl.wait("tensor", "v", v_w)
        if tc >= 2:
            pl.wait("tensor", "v", wtr_v[("wT_v", tc - 2)])
        def f_tr(e, par=par):
            dstp = P[1] if par == 0 else P[3]
            last = e.transpose(dstp[0:NEXP, 0:128], w_t[par][:], ident_f32[:])
            last.then_inc(SM["t"], 1)
        pl.emit("tensor", f_tr)
        t_wt = pl.inc("t")
        wtr_v[("t_wt", tc)] = t_wt
        pl.wait("vector", "t", t_wt)
        def f_we(e, tc=tc, par=par):
            srcp = P[1] if par == 0 else P[3]
            e.tensor_copy(wT_sb[:, tc * 128:(tc + 1) * 128], srcp[0:NEXP, 0:128]).then_inc(SM["v"], 1)
        pl.emit("vector", f_we)
        wtr_v[("wT_v", tc)] = pl.inc("v")
    router_done_v = wtr_v[("wT_v", 7)]

    def f_init(e):
        for hc in range(NIC):
            last = e.tensor_copy(acc[:, hc * TOK:(hc + 1) * TOK], attn_bf[:, hc * TOK:(hc + 1) * TOK])
        last.then_inc(SM["v"], 1)
    pl.emit("vector", f_init)
    acc_init_v = pl.inc("v")

    guw_done_t = {}
    gu_evac_v = {}
    silu_s = {}
    y_evac_v = {}
    wb_evac_v = {}
    wb_mm_t = {}
    y_done_t = {}
    guw_dw = {}
    wd_dw = {}
    wr_dw = {}

    t0_moe = pl.cnt["t"]
    pre_guw_t = {gidx: t0_moe + (gidx // NICC) * 23 + 2 + (gidx % NICC) for gidx in range(NEXP * NICC)}
    pre_y_t = {ee: t0_moe + 23 * (ee + 1) for ee in range(NEXP)}
    pre_wb_t = {ee: t0_moe + ee * 23 + 1 for ee in range(NEXP)}

    def stage_wr(ee):
        sl = f"wr{ee % 2}"
        if ee == 0:
            pl.wait("gpsimd", "v", router_done_v)
        if ee >= 2:
            pl.wait("gpsimd", "t", pre_wb_t[ee - 2])
        def f(e, ee=ee, sl=sl):
            e.dma_start(out=wr_sb[ee % 2][:], in_=wT_sb[ee:ee + 1, :]).then_inc(SM[sl], 16)
        pl.emit("gpsimd", f)
        wr_dw[ee] = (sl, pl.inc(sl, 16))

    stage_wr(0)
    stage_wr(1)
    for ee in range(NEXP):
        for ic in range(NICC):
            gidx = ee * NICC + ic
            slot = gidx % 2
            sl = f"gw{slot}"
            if gidx == 0:
                pl.wait("gpsimd", "v", attn_all_v)
                pl.wait("gpsimd", "t", attn_all_t)
            if gidx >= 2:
                pl.wait("gpsimd", "t", pre_guw_t[gidx - 2])
            def f(e, ee=ee, ic=ic, slot=slot, sl=sl):
                e.dma_start(out=guw_sb[slot][:], in_=guw_h[ee, ic, :, :]).then_inc(SM[sl], 16)
            pl.emit("gpsimd", f)
            guw_dw[gidx] = (sl, pl.inc(sl, 16))
        if ee + 2 < NEXP:
            stage_wr(ee + 2)
        if ee == 0:
            pl.wait("gpsimd", "v", acc_init_v)
        else:
            pl.wait("gpsimd", "t", pre_y_t[ee - 1])
        for ic in range(NICC):
            def f(e, ee=ee, ic=ic):
                e.dma_start(out=wd_sb[:, ic * C:(ic + 1) * C],
                            in_=wd_h[ee, ic * 128:(ic + 1) * 128, :]).then_inc(SM["dwd"], 16)
            pl.emit("gpsimd", f)
            pl.inc("dwd", 16)
        wd_dw[ee] = ("dwd", pl.cnt["dwd"])

    PG = [[P[0], P[1]], [P[4], P[5]]]
    PU = [[P[2], P[3]], [P[6], P[7]]]
    PY = [[P[0], P[1]], [P[2], P[3]]]
    PWB = [P[4], P[5]]

    pl.wait("tensor", "v", attn_all_v)
    for ee in range(NEXP):
        par = ee % 2
        sl, dwv = wr_dw[ee]
        pl.wait("tensor", sl, dwv)
        if ee >= 1:
            pl.wait("tensor", "v", gu_evac_v[(ee - 1) * NICC + NICC - 1])
            pl.wait("tensor", "s", silu_s[(ee - 1) * NICC + NICC - 1])
        def f_wb(e, ee=ee):
            e.matmul(PWB[0][:], lhsT=onesc_bf[:], rhs=wr_sb[ee % 2][:, 0:512], start=True, stop=True)
            last = e.matmul(PWB[1][:], lhsT=onesc_bf[:], rhs=wr_sb[ee % 2][:, 512:TOK], start=True, stop=True)
            last.then_inc(SM["t"], 1)
        pl.emit("tensor", f_wb)
        wb_mm_t[ee] = pl.inc("t")
        assert wb_mm_t[ee] == pre_wb_t[ee]

        pl.wait("vector", "t", wb_mm_t[ee])
        if ee >= 1:
            pass
        def f_wbe(e, par=par):
            e.drain()
            e.tensor_copy(wb_sb[par][:], PQ[2][:]).then_inc(SM["v"], 1)
        pl.emit("vector", f_wbe)
        wb_evac_v[ee] = pl.inc("v")

        for ic in range(NICC):
            gidx = ee * NICC + ic
            sl, dwv = guw_dw[gidx]
            pl.wait("tensor", sl, dwv)
            if ic >= 2:
                pl.wait("tensor", "v", gu_evac_v[gidx - 2])
                pl.wait("tensor", "s", silu_s[gidx - 2])
            if ic == 0 and ee >= 1:
                pl.wait("tensor", "v", y_evac_v[(ee - 1, NIC - 1)])
            if ic == 1:
                pl.wait("tensor", "v", wb_evac_v[ee])
            def f_gu(e, gidx=gidx):
                p2 = gidx % 2
                gp0, gp1 = PG[p2]
                up0, up1 = PU[p2]
                sg = guw_sb[gidx % 2]
                for icc in range(NIC):
                    base = icc * 256
                    lg = sg[:, base:base + 128]
                    lu = sg[:, base + 128:base + 256]
                    st_ = icc == 0
                    sp_ = icc == NIC - 1
                    rx0 = xn_own[:, icc * TOK: icc * TOK + 512]
                    rx1 = xn_own[:, icc * TOK + 512: (icc + 1) * TOK]
                    e.matmul(gp0[:], lhsT=lg, rhs=rx0, start=st_, stop=sp_)
                    e.matmul(gp1[:], lhsT=lg, rhs=rx1, start=st_, stop=sp_)
                    e.matmul(up0[:], lhsT=lu, rhs=rx0, start=st_, stop=sp_)
                    last = e.matmul(up1[:], lhsT=lu, rhs=rx1, start=st_, stop=sp_)
                last.then_inc(SM["t"], 1)
            pl.emit("tensor", f_gu)
            guw_done_t[gidx] = pl.inc("t")
            assert guw_done_t[gidx] == pre_guw_t[gidx]

            pl.wait("scalar", "t", guw_done_t[gidx])
            if gidx >= 1:
                pl.wait("scalar", "v", gu_evac_v[gidx - 1])
            def f_sil(e, gidx=gidx):
                e.drain()
                e.activation(gact[:], (PQ[0] if gidx % 2 == 0 else PQ[2])[:], AF.Silu).then_inc(SM["s"], 1)
            pl.emit("scalar", f_sil)
            silu_s[gidx] = pl.inc("s")

            pl.wait("vector", "s", silu_s[gidx])
            def f_gue(e, gidx=gidx, ee=ee, ic=ic, par=par):
                e.drain()
                e.tensor_mul(gact[:], gact[:], (PQ[1] if gidx % 2 == 0 else PQ[3])[:])
                e.drain()
                e.tensor_mul(gu_sb[par][:, ic * TOK:(ic + 1) * TOK], gact[:], wb_sb[par][:]).then_inc(SM["v"], 1)
            pl.emit("vector", f_gue)
            gu_evac_v[gidx] = pl.inc("v")

        sl, dwv = wd_dw[ee]
        pl.wait("tensor", sl, dwv)
        for hc in range(NIC):
            if hc >= 2:
                pl.wait("tensor", "v", y_evac_v[(ee, hc - 2)])
            if hc == 0:
                for ic in range(NICC):
                    if ic == 0:
                        pl.wait("tensor", "v", gu_evac_v[ee * NICC + 4])
                        pl.wait("tensor", "s", silu_s[ee * NICC + 4])
                    elif ic == 5:
                        pl.wait("tensor", "v", gu_evac_v[ee * NICC + 5])
                    def f_y1(e, ee=ee, ic=ic, par=par):
                        py0, py1 = PY[0]
                        lw = wd_sb[:, ic * C: ic * C + 128]
                        st_ = ic == 0
                        sp_ = ic == NICC - 1
                        e.matmul(py0[:], lhsT=lw, rhs=gu_sb[par][:, ic * TOK: ic * TOK + 512], start=st_, stop=sp_)
                        last = e.matmul(py1[:], lhsT=lw, rhs=gu_sb[par][:, ic * TOK + 512: (ic + 1) * TOK], start=st_, stop=sp_)
                        if sp_:
                            last.then_inc(SM["t"], 1)
                    pl.emit("tensor", f_y1)
                t_y = pl.inc("t")
                pl.wait("vector", "t", t_y)
                def f_ye0(e, ee=ee):
                    e.drain()
                    e.tensor_add(acc[:, 0:TOK], acc[:, 0:TOK], PQ[0][:]).then_inc(SM["v"], 1)
                pl.emit("vector", f_ye0)
                y_evac_v[(ee, 0)] = pl.inc("v")
                continue
            def f_y(e, ee=ee, hc=hc, par=par):
                py0, py1 = PY[hc % 2]
                for ic in range(NICC):
                    lw = wd_sb[:, ic * C + hc * 128: ic * C + (hc + 1) * 128]
                    st_ = ic == 0
                    sp_ = ic == NICC - 1
                    e.matmul(py0[:], lhsT=lw, rhs=gu_sb[par][:, ic * TOK: ic * TOK + 512], start=st_, stop=sp_)
                    last = e.matmul(py1[:], lhsT=lw, rhs=gu_sb[par][:, ic * TOK + 512: (ic + 1) * TOK], start=st_, stop=sp_)
                last.then_inc(SM["t"], 1)
            pl.emit("tensor", f_y)
            t_y = pl.inc("t")
            if hc == NIC - 1:
                y_done_t[ee] = t_y
                assert t_y == pre_y_t[ee]

            pl.wait("vector", "t", t_y)
            def f_ye(e, ee=ee, hc=hc):
                pq = PQ[hc % 2]
                dst = acc[:, hc * TOK:(hc + 1) * TOK]
                e.drain()
                e.tensor_add(dst, dst, pq[:]).then_inc(SM["v"], 1)
            pl.emit("vector", f_ye)
            y_evac_v[(ee, hc)] = pl.inc("v")

    moe_done_v = y_evac_v[(NEXP - 1, NIC - 1)]
    moe_done_t = pl.cnt["t"]

    st_count = 0
    ev_v = {}
    for tc in range(8):
        par = tc % 2
        sl = f"dx{par}"
        if tc < 2:
            pl.wait("sync", "v", moe_done_v)
        else:
            pl.wait("sync", "v", ev_v[tc - 2])
        def f_dx(e, tc=tc, par=par, sl=sl):
            e.dma_start(out=x2_sb[par][:], in_=x_h[tc * 128:(tc + 1) * 128, :]).then_inc(SM[sl], 16)
        pl.emit("sync", f_dx)
        x_dw2 = pl.inc(sl, 16)

        if tc == 0:
            pl.wait("tensor", "v", moe_done_v)
        if tc >= 2:
            pl.wait("tensor", "v", ev_v[tc - 2])
        def f_tr(e, tc=tc, par=par):
            base = 4 * par
            for hc in range(NIC):
                pb = P[base + hc // 4]
                col = (hc % 4) * 128
                last = e.transpose(pb[:, col:col + 128],
                            acc[:, hc * TOK + tc * 128: hc * TOK + tc * 128 + 128], ident_f32[:])
            last.then_inc(SM["t"], 1)
        pl.emit("tensor", f_tr)
        t_tr = pl.inc("t")

        pl.wait("vector", "t", t_tr)
        pl.wait("vector", sl, x_dw2)
        if tc >= 2:
            pl.wait("vector", "st", 16 * tc)
        def f_add(e, tc=tc, par=par):
            base = 4 * par
            e.drain()
            for j in range(4):
                last = e.tensor_add(out_sb[par][:, j * 512:(j + 1) * 512],
                             x2_sb[par][:, j * 512:(j + 1) * 512], P[base + j][:])
            last.then_inc(SM["v"], 1)
        pl.emit("vector", f_add)
        ev_v[tc] = pl.inc("v")

        pl.wait("sync", "v", ev_v[tc])
        def f_st(e, tc=tc, par=par):
            e.dma_start(out=out_h[tc * 128:(tc + 1) * 128, :], in_=out_sb[par][:]).then_inc(SM["st"], 16)
        pl.emit("sync", f_st)
        st_count = pl.inc("st", 16)
        if tc >= 1:
            pass
    pl.wait("sync", "st", st_count)

    with nc.Block() as block:
        @block.sync
        def _(eng):
            for f in pl.q["sync"]:
                f(eng)

        @block.gpsimd
        def _(eng):
            for f in pl.q["gpsimd"]:
                f(eng)

        @block.tensor
        def _(eng):
            for f in pl.q["tensor"]:
                f(eng)

        @block.scalar
        def _(eng):
            for f in pl.q["scalar"]:
                f(eng)

        @block.vector
        def _(eng):
            for f in pl.q["vector"]:
                f(eng)

    for g in pl._sem_guards:
        g.__exit__(None, None, None)
    return nc


_PERM = {}
def _perms():
    if _PERM:
        return _PERM
    q = [np.arange(i * 512, (i + 1) * 512) for i in range(4)]
    _PERM[0] = np.concatenate([q[0], q[3], q[1], q[2]])
    _PERM[1] = np.concatenate([q[1], q[2], q[0], q[3]])
    return _PERM


_CACHE = {}
def kernel(L0_attn_x, L0_attn_wq, L0_attn_wk, L0_attn_wv, L0_attn_tau, L0_attn_nu,
           L0_ffn_gate_w, L0_ffn_w_gate, L0_ffn_w_up, L0_ffn_w_down):
    x = np.asarray(L0_attn_x, dtype=np.float32)
    wq = np.asarray(L0_attn_wq, dtype=np.float32)
    wk = np.asarray(L0_attn_wk, dtype=np.float32)
    wv = np.asarray(L0_attn_wv, dtype=np.float32)
    tau = np.asarray(L0_attn_tau, dtype=np.float32).reshape(1, 1)
    nu = np.asarray(L0_attn_nu, dtype=np.float32).reshape(1, 1)
    gw = np.asarray(L0_ffn_gate_w, dtype=np.float32)
    wg = np.asarray(L0_ffn_w_gate, dtype=np.float32)
    wu = np.asarray(L0_ffn_w_up, dtype=np.float32)
    wd = np.asarray(L0_ffn_w_down, dtype=np.float32)

    perms = _perms()
    wqt = np.ascontiguousarray(wq.T)
    wkt = np.ascontiguousarray(wk.T)
    wvt = np.ascontiguousarray(wv.T)
    gwt = np.ascontiguousarray(gw.T.reshape(NIC, 128, NEXP).transpose(1, 0, 2).reshape(128, NIC * NEXP))
    wgr = wg.reshape(NEXP, NIC, 128, NICC, 128)
    wur = wu.reshape(NEXP, NIC, 128, NICC, 128)
    guw = np.ascontiguousarray(
        np.stack([wgr, wur], axis=3).transpose(0, 4, 2, 1, 3, 5).reshape(NEXP, NICC, 128, NIC * 2 * 128))
    wdc = np.ascontiguousarray(wd)

    kk = np.arange(128)[:, None]
    qq = np.arange(128)[None, :]
    tri = np.where(kk <= qq, 0.0, NEG_BIG).astype(ml_dtypes.bfloat16)
    m8 = {}
    m12 = {}
    m8[0] = np.concatenate([np.zeros((128, 512)), np.ones((128, 512))], axis=1).astype(ml_dtypes.bfloat16)
    m8[1] = np.ones((128, 1024), dtype=ml_dtypes.bfloat16)
    m12[0] = np.ones((128, 512), dtype=ml_dtypes.bfloat16)
    m12[1] = np.zeros((128, 512), dtype=ml_dtypes.bfloat16)

    if "nc" not in _CACHE:
        _CACHE["nc"] = build_nc()
    nc = _CACHE["nc"]

    in_maps = []
    for c in range(8):
        b, par = c // 2, c % 2
        xc = np.ascontiguousarray(x[b][perms[par]])
        in_maps.append(dict(
            x=xc, wqt=wqt, wkt=wkt, wvt=wvt, gwt=gwt, tau=tau, nu=nu,
            guw=guw, wd=wdc, trimask=tri, mask8=m8[par], mask12=m12[par]))

    res = run_bass_kernel_spmd(nc, in_maps, core_ids=list(range(8)))

    out = np.empty((B, S, C), dtype=np.float32)
    for c in range(8):
        b, par = c // 2, c % 2
        out[b][perms[par][:TOK]] = res.results[c]["out"]
    return out
